# version 09 (final)
# baseline (speedup 1.0000x reference)
import math
from contextlib import ExitStack

import numpy as np
import concourse.bass as bass
import concourse.mybir as mybir
from concourse.bass_utils import run_bass_kernel_spmd

F32 = mybir.dt.float32
F32R = mybir.dt.float32r
BF16 = mybir.dt.bfloat16
I32 = mybir.dt.int32
I16 = mybir.dt.int16
U16 = mybir.dt.uint16
AF = mybir.ActivationFunctionType
ALU = mybir.AluOpType
AX = mybir.AxisListType

B, T, E = 2, 4096, 2048
TL = 1024
NE_CH = 16
H_D = 128
NHEAD_CORE = 4
SQRT_D_INV = 1.0 / math.sqrt(128.0)
ROPE_BASE = 1000000.0
LN_EPS = 1e-5
NS_COEFS = [(4.0848, -6.8946, 2.927), (3.9505, -6.3029, 2.6377),
            (3.7418, -5.5913, 2.3037), (2.8769, -3.1427, 1.2046),
            (2.8366, -3.0525, 1.2012)]
BASE_LR_INV = 0.01 + math.log(-math.expm1(-0.01))
FFN_CHUNKS = [6, 6, 6, 6, 5, 5, 5, 5]
FFN_I0 = [128 * sum(FFN_CHUNKS[:c]) for c in range(8)]
I_FULL = 5632

STAGE = "full"


class PB:
    ENGINES = ("sync", "tensor", "vector", "scalar", "gpsimd")

    def __init__(self, nc, ctx):
        self.nc = nc
        self.ctx = ctx
        self.ops = []
        self.cnt = {}
        self.sems = {}

    def sem(self, name):
        if name not in self.sems:
            self.sems[name] = self.ctx.enter_context(self.nc.semaphore(name))
            self.cnt[name] = 0
        return self.sems[name]

    def now(self, name):
        self.sem(name)
        return self.cnt[name]

    def emit(self, eng, fn, waits=(), inc=None, inc_amt=1):
        waits = tuple((s, v) for (s, v) in waits if v > 0)
        for s, _ in waits:
            self.sem(s)
        if inc is not None:
            self.sem(inc)
            self.cnt[inc] += inc_amt
        self.ops.append((eng, fn, waits, (inc, inc_amt) if inc else None))
        return self.cnt[inc] if inc else None

    def wait_only(self, eng, waits):
        self.emit(eng, None, waits)

    def dma(self, eng, out, in_, waits=(), inc=None):
        return self.emit(eng, (lambda e, o=out, i=in_: e.dma_start(out=o, in_=i)),
                         waits, inc, 16)

    def flush(self, name=None):
        ops = self.ops
        self.ops = []
        sems = self.sems
        with self.nc.Block(name) as block:
            for eng in self.ENGINES:
                eng_ops = [o for o in ops if o[0] == eng]
                if not eng_ops:
                    continue

                def body(e, eng_ops=eng_ops):
                    for _, fn, waits, inc in eng_ops:
                        for s, v in waits:
                            e.wait_ge(sems[s], v)
                        if fn is None:
                            continue
                        instr = fn(e)
                        if inc is not None:
                            instr.then_inc(sems[inc[0]], inc[1])
                getattr(block, eng)(body)



def _prep_core_inputs(c, I):
    b, hg = c // 4, c % 4
    d = {}
    x = I["L0_attn_x"]
    d["x_tok"] = np.ascontiguousarray(x[b, 1024 * hg:1024 * hg + 1024])
    d["wqT"] = np.ascontiguousarray(I["L0_attn_wq"].T[:, 512 * hg:512 * hg + 512])
    d["wkT"] = np.ascontiguousarray(I["L0_attn_wk"].T[:, 512 * hg:512 * hg + 512])
    d["wvT"] = np.ascontiguousarray(I["L0_attn_wv"].T[:, 512 * hg:512 * hg + 512])
    d["woT"] = np.ascontiguousarray(I["L0_attn_wo"].T[512 * hg:512 * hg + 512, :])
    d["wdT"] = np.ascontiguousarray(I["L0_ffn_wd"].transpose(0, 2, 1))
    d["wuT"] = np.ascontiguousarray(I["L0_ffn_wu"].transpose(0, 2, 1))
    d["wsdT"] = np.ascontiguousarray(I["L0_ffn_ws_d"].T)
    d["wsuT"] = np.ascontiguousarray(I["L0_ffn_ws_u"].T)
    d["gateT"] = np.ascontiguousarray(I["L0_ffn_gate_w"].T)
    d["bd"] = np.ascontiguousarray(I["L0_ffn_bd"].reshape(8, 4, 128).transpose(2, 0, 1).reshape(128, 32))
    d["bu"] = np.ascontiguousarray(I["L0_ffn_bu"])
    d["bsd"] = np.ascontiguousarray(I["L0_ffn_bs_d"].reshape(4, 128).T)
    d["bsu"] = np.ascontiguousarray(I["L0_ffn_bs_u"].reshape(1, 2048))
    Wp = (I["L1_attn_W_qkv"].reshape(E, 16, 3, 128).transpose(0, 2, 1, 3)
          .reshape(E, 6144).astype(np.float32))
    u = Wp.view(np.uint32)
    rnd = ((u >> 16) & 1) + 0x7FFF
    d["wqkv"] = np.ascontiguousarray(((u + rnd) >> 16).astype(np.uint16))
    heads = list(range(4 * hg, 4 * hg + 4))
    cols = ([I["L1_attn_W_lr"][:, 16 * j + h] for j in range(3) for h in heads]
            + [I["L1_attn_W_m"][:, h] for h in heads]
            + [I["L1_attn_W_s"][:, h] for h in heads])
    d["wsm"] = np.ascontiguousarray(np.stack(cols, axis=1))
    bias = np.array([BASE_LR_INV] * 12
                    + [I["L1_attn_b_m"][h] for h in heads]
                    + [I["L1_attn_b_s"][h] for h in heads], np.float32)
    d["bsm"] = bias.reshape(1, 20)
    d["lng"] = np.ascontiguousarray(I["L1_attn_ln_g"].reshape(1, 128))
    d["lnb"] = np.ascontiguousarray(I["L1_attn_ln_b"].reshape(1, 128))
    d["w0"] = np.ascontiguousarray(I["L1_attn_w0"][heads])
    d["w1"] = np.ascontiguousarray(I["L1_attn_w1"][heads])
    d["w2"] = np.ascontiguousarray(I["L1_attn_w2"][heads])
    d["w0T"] = np.ascontiguousarray(I["L1_attn_w0"][heads].transpose(0, 2, 1))
    d["w1T"] = np.ascontiguousarray(I["L1_attn_w1"][heads].transpose(0, 2, 1))
    d["w2T"] = np.ascontiguousarray(I["L1_attn_w2"][heads].transpose(0, 2, 1))
    d["wo1"] = np.ascontiguousarray(I["L1_attn_W_o"][512 * hg:512 * hg + 512, :])
    i0, ni = FFN_I0[c], 128 * FFN_CHUNKS[c]

    def pk_rep(p):
        pt = np.ascontiguousarray(p.T)
        w, r = pt.shape
        u = pt.view(np.uint16).reshape(w, r, 2).transpose(0, 2, 1).reshape(2 * w, r)
        return np.ascontiguousarray(np.repeat(u, 4, axis=0))

    gup = I["L1_ffn_gate_up_packed"]
    d["gpk"] = pk_rep(gup[i0:i0 + ni])
    d["upk"] = pk_rep(gup[I_FULL + i0:I_FULL + i0 + ni])
    dwn = I["L1_ffn_down_packed"]
    d["dpk"] = pk_rep(dwn[:, i0 // 8:(i0 + ni) // 8].T)
    dpt = np.ascontiguousarray(dwn[:, i0 // 8:(i0 + ni) // 8].T)
    w_, r_ = dpt.shape
    du = dpt.view(np.uint16).reshape(w_, r_, 2).transpose(0, 2, 1).reshape(2 * w_, r_)
    d["dpk"] = np.ascontiguousarray(np.repeat(du, 4, axis=0))
    d["gsc"] = np.ascontiguousarray(I["L1_ffn_gate_up_scales"][:, i0:i0 + ni])
    d["usc"] = np.ascontiguousarray(I["L1_ffn_gate_up_scales"][:, I_FULL + i0:I_FULL + i0 + ni])
    d["dsc"] = np.ascontiguousarray(I["L1_ffn_down_scales"][i0 // 128:(i0 + ni) // 128, :])
    return d



def build(ni):
    nc = bass.Bass("TRN2", target_bir_lowering=False)
    g = {}

    def din(name, shape, dt):
        g[name] = nc.dram_tensor(name, shape, dt, kind="ExternalInput").ap()

    din("x_tok", [TL, E], F32)
    din("wqT", [E, 512], F32R)
    din("wkT", [E, 512], F32R)
    din("wvT", [E, 512], F32R)
    din("woT", [512, E], F32R)
    din("wdT", [8, E, 512], F32R)
    din("wuT", [8, 512, E], F32R)
    din("wsdT", [E, 512], F32R)
    din("wsuT", [512, E], F32R)
    din("gateT", [E, 8], F32R)
    din("bd", [128, 32], F32)
    din("bu", [8, E], F32R)
    din("bsd", [128, 4], F32)
    din("bsu", [1, E], F32)
    din("wqkv", [E, 6144], U16)
    din("wsm", [E, 20], F32R)
    din("bsm", [1, 20], F32)
    din("lng", [1, 128], F32)
    din("lnb", [1, 128], F32)
    din("w0", [4, 128, 128], F32)
    din("w1", [4, 128, 128], F32)
    din("w2", [4, 128, 128], F32)
    din("w0T", [4, 128, 128], F32)
    din("w1T", [4, 128, 128], F32)
    din("w2T", [4, 128, 128], F32)
    din("wo1", [512, E], F32R)
    din("gpk", [2048, ni], U16)
    din("upk", [2048, ni], U16)
    din("dpk", [ni, 2048], U16)
    din("gsc", [16, ni], mybir.dt.float16)
    din("usc", [16, ni], mybir.dt.float16)
    din("dsc", [ni // 128, 2048], mybir.dt.float16)
    out = nc.dram_tensor("out", [TL, E], F32, kind="ExternalOutput").ap()
    g["out"] = out

    def dsc(name, shape, dt):
        g[name] = nc.dram_tensor(name, shape, dt, kind="Internal").ap()

    dsc("ag0_in", [E, TL], F32)
    dsc("ag0_out", [4 * E, TL], F32)
    dsc("comb_d", [8, TL], F32)
    dsc("s0_part", [T, E], F32)
    dsc("s0_out", [TL, E], F32)
    dsc("g0_d", [TL, E], F32)
    dsc("x1_d", [TL, E], F32)
    dsc("ag1r_in", [E, TL], F32)
    dsc("ag1r_out", [4 * E, TL], F32)
    dsc("ag1b_in", [E, TL], BF16)
    dsc("ag1b_out", [8 * E, TL], BF16)
    dsc("q1_d", [TL, E], BF16)
    dsc("k1_d", [TL, E], BF16)
    dsc("v1_d", [TL, E], BF16)
    dsc("smalls_d", [T, 20], F32)
    dsc("y1_part", [2 * T, E], BF16)
    dsc("y1_out", [TL, E], BF16)
    dsc("s1_part", [T, E], BF16)
    dsc("s1_out", [TL, E], BF16)
    dsc("scg_d", [16, ni], F32)
    dsc("scu_d", [16, ni], F32)
    dsc("scd_d", [ni // 128, E], F32)

    ctx = ExitStack()
    pb = PB(nc, ctx)

    def sb(name, cols, dt, rows=128):
        return ctx.enter_context(nc.sbuf_tensor(name, [rows, cols], dt))

    def ps(name, cols, dt=F32, rows=128):
        return ctx.enter_context(nc.psum_tensor(name, [rows, cols], dt))

    with ctx:
        build_phases(nc, pb, g, sb, ps, ni)
    return nc


def build_phases(nc, pb, g, sb, ps, ni):
    ctx_outer = pb.ctx

    def sbl(stack, name, cols, dt, rows=128):
        return stack.enter_context(nc.sbuf_tensor(name, [rows, cols], dt))

    ident_b = sb("ident_b", 128, BF16)
    ident_f = sb("ident_f", 128, F32)
    ident_r = sb("ident_r", 128, F32R)
    ones_f = sb("ones_f", 128, F32)
    ones_r = sb("ones_r", 128, F32R)
    mask_d = sb("mask_d", 128, F32)
    io_a = sb("io_a", 128, I32)
    io_b = sb("io_b", 128, I32)
    col_tmp = sb("col_tmp", 32, F32)

    pbig = ps("pbig", 4096)
    pp = [pbig[:, 512 * i:512 * (i + 1)] for i in range(8)]

    cpool = sb("cpool", 8, F32)
    cmap = {}

    def ccol(val):
        assert val in cmap, f"const {val} not pre-seeded"
        return cmap[val][0]

    for _cv in (1.0, LN_EPS, 1e-6, 1e-7):
        _j = len(cmap)
        pb.emit("gpsimd", lambda e, j=_j, val=_cv: e.memset(cpool[:, j:j + 1], float(val)),
                inc="gp")
        cmap[_cv] = (cpool[:, _j:_j + 1], pb.now("gp"))
    g_const = pb.now("gp")
    for _eng in ("scalar", "vector", "tensor"):
        pb.wait_only(_eng, [("gp", g_const)])

    pb.emit("gpsimd", lambda e: e.iota(io_a[:], [[1, 128]], channel_multiplier=0),
            inc="gp")
    pb.emit("gpsimd", lambda e: e.iota(io_b[:], [[0, 128]], channel_multiplier=1),
            inc="gp")
    gp_io = pb.now("gp")
    pb.emit("vector", lambda e: e.tensor_tensor(ident_f[:], io_a[:], io_b[:], op=ALU.is_equal),
            waits=[("gp", gp_io)], inc="vec")
    vi = pb.now("vec")
    pb.emit("vector", lambda e: e.tensor_copy(ident_b[:], ident_f[:]),
            waits=[("vec", vi)], inc="vec")
    pb.emit("vector", lambda e: e.tensor_copy(ident_r[:], ident_f[:]),
            waits=[("vec", vi)], inc="vec")
    pb.emit("vector", lambda e: e.memset(ones_f[:], 1.0), inc="vec")
    vo = pb.now("vec")
    pb.emit("vector", lambda e: e.tensor_copy(ones_r[:], ones_f[:]),
            waits=[("vec", vo)], inc="vec")
    rotT = sb("rotT", 128, F32R)
    pb.emit("vector", lambda e: e.tensor_scalar(
        mask_d[:], io_a[:], 64, None, op0=ALU.add), waits=[("gp", gp_io)], inc="vec")
    _v = pb.now("vec")
    pb.emit("vector", lambda e: e.tensor_tensor(rotT[:], mask_d[:], io_b[:], op=ALU.is_equal),
            waits=[("vec", _v)], inc="vec")
    _v = pb.now("vec")
    pb.emit("vector", lambda e: e.tensor_scalar(
        mask_d[:], io_a[:], -64, None, op0=ALU.add), waits=[("vec", _v)], inc="vec")
    _v = pb.now("vec")
    pb.emit("vector", lambda e: e.tensor_tensor(mask_d[:], mask_d[:], io_b[:], op=ALU.is_equal),
            waits=[("vec", _v)], inc="vec")
    _v = pb.now("vec")
    pb.emit("vector", lambda e: e.scalar_tensor_tensor(
        rotT[:], rotT[:], -1.0, mask_d[:], op0=ALU.mult, op1=ALU.add),
        waits=[("vec", _v)], inc="vec")
    _v = pb.now("vec")
    pb.emit("vector", lambda e: e.tensor_tensor(mask_d[:], io_a[:], io_b[:], op=ALU.is_gt),
            waits=[("vec", _v)], inc="vec")
    vm = pb.now("vec")
    pb.emit("vector", lambda e: e.tensor_scalar(mask_d[:], mask_d[:], -1e30, None, op0=ALU.mult),
            waits=[("vec", vm)], inc="vec")

    ctx01 = ExitStack()
    pb.ctx.enter_context(ctx01)
    cosA = sbl(ctx01, "cosA", T, F32R)
    sinA = sbl(ctx01, "sinA", T, F32R)

    with ExitStack() as c0:
        sm_tmp = sbl(c0, "sm_tmp", 4096, F32)
        invc = sbl(c0, "invc", 1, F32)
        invc_i = sbl(c0, "invc_i", 1, I32)
        tix = sbl(c0, "tix", T, F32)
        tix2 = sbl(c0, "tix2", T, F32)
        pb.emit("gpsimd", lambda e: e.iota(invc_i[:], [[0, 1]], channel_multiplier=1), inc="gp")
        gp_i = pb.now("gp")
        pb.emit("gpsimd", lambda e: e.iota(tix[:], [[1, T]], channel_multiplier=0,
                                           allow_small_or_imprecise_dtypes=True), inc="gp")
        gp_t = pb.now("gp")
        pb.emit("vector", lambda e: e.tensor_scalar(invc_i[:], invc_i[:], 63, None, op0=ALU.bitwise_and),
                waits=[("gp", gp_i)], inc="vec")
        v1 = pb.now("vec")
        pb.emit("vector", lambda e: e.tensor_copy(invc[:], invc_i[:]),
                waits=[("vec", v1)], inc="vec")
        v1 = pb.now("vec")
        pb.emit("scalar", lambda e: e.activation(invc[:], invc[:], AF.Exp,
                                                 scale=-math.log(ROPE_BASE) / 64.0),
                waits=[("vec", v1)], inc="scl")
        s1 = pb.now("scl")

        def rope_table(dst, phase, negate_lo, wait_scl=0):
            A = sm_tmp[:, 0:T]
            kf_ = sm_tmp[:, 0:T]
            dst = dst[:]
            dsti = dst.bitcast(I32)
            dstf = dst.bitcast(F32)
            v0 = pb.now("vec")
            pb.emit("vector", lambda e: e.tensor_scalar(
                A, tix[:], invc[:, 0:1], phase + math.pi, op0=ALU.mult, op1=ALU.add),
                waits=[("gp", gp_t), ("scl", max(s1, wait_scl)), ("vec", v0)], inc="vec")
            v = pb.now("vec")
            pb.emit("vector", lambda e: e.tensor_scalar(
                dsti, A, 1.0 / (2 * math.pi), None, op0=ALU.mult),
                waits=[("vec", v)], inc="vec")
            v = pb.now("vec")
            pb.emit("vector", lambda e: e.tensor_copy(dstf, dsti),
                    waits=[("vec", v)], inc="vec")
            v = pb.now("vec")
            pb.emit("vector", lambda e: e.scalar_tensor_tensor(
                tix2[:], dstf, 2 * math.pi, A, op0=ALU.mult, op1=ALU.is_gt),
                waits=[("vec", v)], inc="vec")
            v = pb.now("vec")
            pb.emit("vector", lambda e: e.tensor_tensor(
                dstf, dstf, tix2[:], op=ALU.subtract),
                waits=[("vec", v)], inc="vec")
            v = pb.now("vec")
            pb.emit("vector", lambda e: e.scalar_tensor_tensor(
                A, dstf, -2 * math.pi, A, op0=ALU.mult, op1=ALU.add),
                waits=[("vec", v)], inc="vec")
            v = pb.now("vec")
            pb.emit("vector", lambda e: e.tensor_scalar(
                A, A, -math.pi, math.pi, op0=ALU.add, op1=ALU.min),
                waits=[("vec", v)], inc="vec")
            v = pb.now("vec")
            pb.emit("vector", lambda e: e.tensor_scalar(
                A, A, -math.pi, None, op0=ALU.max),
                waits=[("vec", v)], inc="vec")
            v = pb.now("vec")
            pb.emit("scalar", lambda e: e.activation(dst[:, :], A, AF.Sin),
                    waits=[("vec", v)], inc="scl")
            return pb.now("scl")

        _sdone = rope_table(sinA, 0.0, False)
        s_tab = rope_table(cosA, math.pi / 2.0, False, wait_scl=_sdone)

        xt = sbl(c0, "xt", 2 * E, F32)
        xn_tm = sbl(c0, "xn_tm", E, F32R)
        lnscr = sbl(c0, "lnscr", E, F32)
        xn_fm = sbl(c0, "xn_fm", NE_CH * TL, F32R)
        dbg_dsc = 0
        for tt in range(8):
            buf = xt[:, (tt % 2) * E:(tt % 2) * E + E]
            w_prev = [("scl", pb.now("scl"))] if tt >= 1 else [("scl", s_tab)]
            ld = pb.dma("sync", buf, g["x_tok"][128 * tt:128 * tt + 128, :],
                        waits=w_prev, inc="dls")
            v0 = pb.now("vec")
            pb.emit("vector", lambda e, b=buf: e.tensor_reduce(
                col_tmp[:, 0:1], b, axis=AX.X, op=ALU.add),
                waits=[("dls", ld), ("vec", v0)], inc="vec")
            pb.emit("vector", lambda e, b=buf: e.tensor_tensor(
                lnscr[:, :], b, b, op=ALU.mult),
                waits=[("dls", ld), ("vec", v0)], inc="vec")
            _vsq = pb.now("vec")
            pb.emit("vector", lambda e: e.tensor_reduce(
                col_tmp[:, 1:2], lnscr[:, :], axis=AX.X, op=ALU.add),
                waits=[("vec", _vsq)], inc="vec")
            v = pb.now("vec"); s = pb.now("scl")
            pb.emit("vector", lambda e: e.tensor_scalar(
                col_tmp[:, 0:1], col_tmp[:, 0:1], 1.0 / E, None, op0=ALU.mult),
                waits=[("vec", v)], inc="vec")
            v = pb.now("vec")
            pb.emit("vector", lambda e: e.tensor_scalar(
                col_tmp[:, 2:3], col_tmp[:, 0:1], col_tmp[:, 0:1], None, op0=ALU.mult),
                waits=[("vec", v)], inc="vec")
            v = pb.now("vec")
            pb.emit("vector", lambda e: e.scalar_tensor_tensor(
                col_tmp[:, 1:2], col_tmp[:, 1:2], 1.0 / E, col_tmp[:, 2:3],
                op0=ALU.mult, op1=ALU.subtract),
                waits=[("vec", v), ("scl", s)], inc="vec")
            v = pb.now("vec")
            pb.emit("scalar", lambda e: e.activation(
                col_tmp[:, 3:4], col_tmp[:, 1:2], AF.Ln, bias=ccol(LN_EPS)),
                waits=[("vec", v)], inc="scl")
            s = pb.now("scl")
            pb.emit("scalar", lambda e: e.activation(
                col_tmp[:, 3:4], col_tmp[:, 3:4], AF.Exp, scale=0.5),
                waits=[("scl", s)], inc="scl")
            s = pb.now("scl")
            pb.emit("vector", lambda e: e.reciprocal(col_tmp[:, 3:4], col_tmp[:, 3:4]),
                    waits=[("scl", s)], inc="vec")
            v = pb.now("vec")
            pb.emit("vector", lambda e, b=buf: e.tensor_scalar(
                xn_tm[:, :], b, col_tmp[:, 0:1], col_tmp[:, 3:4],
                op0=ALU.subtract, op1=ALU.mult),
                waits=[("vec", v), ("dsc", dbg_dsc)], inc="vec")
            v = pb.now("vec")
            if STAGE == "xn0":
                dbg_dsc = pb.dma("scalar", g["out"][128 * tt:128 * tt + 128, :],
                                 xn_tm[:].bitcast(F32), waits=[("vec", v)], inc="dsc")
            for ec in range(16):
                bank = pp[ec % 2]
                wait_evac = [("scl", pb.now("scl"))] if ec >= 2 else []
                pb.emit("tensor", lambda e, ec=ec, bank=bank: e.transpose(
                    bank[:, 0:128].bitcast(F32R), xn_tm[:, 128 * ec:128 * ec + 128], ident_r[:]),
                    waits=[("vec", v)] + wait_evac, inc="tns")
                tw = pb.now("tns")
                pb.emit("scalar", lambda e, ec=ec, tt=tt, bank=bank: e.copy(
                    xn_fm[:, TL * ec + 128 * tt: TL * ec + 128 * tt + 128],
                    bank[:, 0:128].bitcast(F32R)),
                    waits=[("tns", tw)], inc="scl")
        sl = pb.now("scl")
        pb.dma("scalar",
               g["ag0_in"].rearrange("(c p) t -> p c t", p=128),
               xn_fm[:].rearrange("p (c t) -> p c t", t=TL).bitcast(F32),
               waits=[("scl", sl)], inc="dsc")
        pb.emit("gpsimd", lambda e: e.collective_compute(
            "AllGather", ALU.bypass, replica_groups=[[0, 1, 2, 3], [4, 5, 6, 7]],
            ins=[g["ag0_in"].opt()], outs=[g["ag0_out"].opt()]),
            waits=[("dsc", pb.now("dsc"))], inc="cc")
        cc_ag0 = pb.now("cc")
        pb.flush("ph0")

    def finish(extra=()):
        waits = [(s, pb.now(s)) for s in sorted(pb.cnt)
                 if s.startswith(("d",)) and pb.cnt[s] > 0]
        pb.wait_only("scalar", waits)
        pb.flush("fin")

    if STAGE == "xn0":
        finish()
        return


    cc_rs0 = cc_ag0
    if STAGE != "moe":
      with ExitStack() as c1a:
        Ob = [sbl(c1a, "Ob%d" % j, T, BF16) for j in range(4)]
        with ExitStack() as c1:
            wtb = sbl(c1, "wtb", 3 * 16 * 128, F32R)
            qf = sbl(c1, "qf", T, F32R)
            kf = sbl(c1, "kf", T, F32R)
            vb = sbl(c1, "vb", T, BF16)
            vtm = sbl(c1, "vtm", T, BF16)
            rhsb = sbl(c1, "rhsb", 2 * 1024, F32R)
            Pq = sbl(c1, "Pq", 4 * T, BF16)
            PTb = sbl(c1, "PTb", 2 * 512, BF16)
            diag = sbl(c1, "diag", 512, BF16)
            sumc = sbl(c1, "sumc", 8, F32)
            rt1t = sbl(c1, "rt1t", 1024, F32R)
            rt2t = sbl(c1, "rt2t", 1024, F32R)
            rt1 = rt1t[:, :]
            rt2 = rt2t
            pb.wait_only("tensor", [("cc", cc_ag0)])
            pb.wait_only("sync", [("cc", cc_ag0)])

            def wtile(slot, ec):
                return wtb[:, (slot * 16 + ec) * 128:(slot * 16 + ec) * 128 + 128]

            mm_last = {0: 0, 1: 0}
            for j in range(1 if STAGE in ('qk0', 'qraw') else 4):
                wl = 0
                for nm, slot in (("wqT", 0), ("wkT", 1), ("wvT", 2)):
                    wl = pb.dma(
                        "sync",
                        wtb[:].rearrange("p (s c f) -> p s c f", s=3, f=128)[:, slot],
                        g[nm][:, 128 * j:128 * j + 128].rearrange("(c p) f -> p c f", p=128),
                        waits=[("tns", pb.now("tns"))] if j > 0 else [],
                        inc="dls")
                for tq in range(4):
                    evac_gate = pb.now("scl")
                    for ec in range(16):
                        rr = rhsb[:, (ec % 2) * 1024:(ec % 2) * 1024 + 1024]
                        rl = pb.dma("sync", rr,
                                    g["ag0_out"][2048 * tq + 128 * ec:2048 * tq + 128 * ec + 128, :].bitcast(F32R),
                                    waits=[("tns", mm_last[ec % 2])], inc="dq%d" % (ec % 2))
                        for s_i, bank0 in ((0, 0), (1, 2), (2, 4)):
                            for tb in range(2):
                                pb.emit("tensor", lambda e, s_i=s_i, ec=ec, rr=rr, tb=tb, bank0=bank0: e.matmul(
                                    pp[bank0 + tb], wtile(s_i, ec), rr[:, 512 * tb:512 * tb + 512],
                                    start=(ec == 0), stop=(ec == 15)),
                                    waits=[("dq%d" % (ec % 2), rl), ("dls", wl), ("scl", evac_gate)],
                                    inc="tns")
                        mm_last[ec % 2] = pb.now("tns")
                    tdone = pb.now("tns")
                    for s_i, buf, dt_ in ((0, qf, F32R), (1, kf, F32R), (2, vb, BF16)):
                        for tb in range(2):
                            pb.emit("scalar", lambda e, s_i=s_i, tb=tb, buf=buf, tq=tq, dt_=dt_: e.copy(
                                buf[:, 1024 * tq + 512 * tb:1024 * tq + 512 * tb + 512],
                                pp[2 * s_i + tb].bitcast(dt_) if dt_ == F32R else pp[2 * s_i + tb]),
                                waits=[("tns", tdone)], inc="scl")
                    ev = pb.now("scl")
                    for buf in ((qf, kf) if STAGE != "qraw" else ()):
                        sl_ = slice(1024 * tq, 1024 * tq + 1024)
                        v0 = pb.now("vec")
                        rot_mm = []
                        for hb_ in range(2):
                            pb.emit("tensor", lambda e, buf=buf, tq=tq, hb_=hb_: e.matmul(
                                pp[6 + hb_], rotT[:],
                                buf[:, 1024 * tq + 512 * hb_:1024 * tq + 512 * hb_ + 512],
                                start=True, stop=True),
                                waits=[("scl", ev), ("vec", v0)], inc="tns")
                            rot_mm.append(pb.now("tns"))
                        pb.emit("vector", lambda e, buf=buf, sl_=sl_: e.tensor_tensor(
                            rt1, buf[:, sl_], cosA[:, sl_], op=ALU.mult),
                            waits=[("scl", ev), ("vec", v0)], inc="vec")
                        for hb_ in range(2):
                            pb.emit("vector", lambda e, buf=buf, tq=tq, hb_=hb_: e.tensor_tensor(
                                rt2[:, 512 * hb_:512 * hb_ + 512], pp[6 + hb_],
                                sinA[:, 1024 * tq + 512 * hb_:1024 * tq + 512 * hb_ + 512],
                                op=ALU.mult),
                                waits=[("tns", rot_mm[hb_]), ("vec", v0)], inc="vec")
                        v1_ = pb.now("vec")
                        pb.emit("vector", lambda e, buf=buf, sl_=sl_: e.tensor_tensor(
                            buf[:, sl_], rt1, rt2[:, :], op=ALU.add),
                            waits=[("vec", v1_)], inc="vec")
                vrope = pb.now("vec")
                if STAGE == "qraw" and j == 0:
                    pb.dma("scalar", g["out"][0:128, :],
                           qf[:, 0:2048].bitcast(F32), waits=[("scl", pb.now("scl"))], inc="dsc")
                    pb.dma("scalar", g["out"][128:256, :],
                           wtb[:, 0:2048].bitcast(F32), waits=[("scl", pb.now("scl"))], inc="dsc")
                    pb.dma("scalar", g["out"][256:384, :, ],
                           rhsb[:, 0:2048].bitcast(F32), waits=[("scl", pb.now("scl"))], inc="dsc")
                    pb.dma("scalar", g["out"][384:512, :], cosA[:, 0:2048].bitcast(F32), inc="dsc")
                    pb.dma("scalar", g["out"][512:640, :], sinA[:, 0:2048].bitcast(F32), inc="dsc")
                    pb.dma("scalar", g["out"][640:768, :], rotT[:].broadcast_to([128, 128]).bitcast(F32), inc="dsc") if False else None
                    pb.dma("scalar", g["out"][640:768, 0:128], rotT[:].bitcast(F32), inc="dsc")
                if STAGE == "qk0" and j == 0:
                    for half in range(2):
                        pb.dma("scalar", g["out"][128 * half:128 * half + 128, :],
                               qf[:, 2048 * half:2048 * half + 2048].bitcast(F32),
                               waits=[("vec", vrope)], inc="dsc")
                        pb.dma("scalar", g["out"][256 + 128 * half:256 + 128 * half + 128, :],
                               kf[:, 2048 * half:2048 * half + 2048].bitcast(F32),
                               waits=[("vec", vrope)], inc="dsc")
                tr_ev = {0: 0, 1: 0}
                for kt in range(32):
                    pout = pp[6 + (kt % 2)][:, 0:64].bitcast(BF16)
                    pb.emit("tensor", lambda e, kt=kt, pout=pout: e.transpose(
                        pout, vb[:, 128 * kt:128 * kt + 128], ident_b[:]),
                        waits=[("vec", vrope), ("scl", tr_ev[kt % 2])], inc="tns")
                    tv = pb.now("tns")
                    pb.emit("scalar", lambda e, kt=kt, pout=pout: e.copy(
                        vtm[:, 128 * kt:128 * kt + 128], pout),
                        waits=[("tns", tv)], inc="scl")
                    tr_ev[kt % 2] = pb.now("scl")
                o_evac = pb.now("scl")
                prev_av = pb.now("tns")
                for gq in range(8):
                    exp_s = 0
                    for ql in range(4):
                        i = 4 * gq + ql
                        klen = 128 * (i + 1)
                        nbk = (i + 4) // 4
                        prev_s = pb.now("scl")
                        prev_v = pb.now("vec")
                        for kb in range(nbk):
                            w = min(512, klen - 512 * kb)
                            pb.emit("tensor", lambda e, i=i, kb=kb, w=w: e.matmul(
                                pbig[:, 512 * kb:512 * kb + w],
                                qf[:, 128 * i:128 * i + 128],
                                kf[:, 512 * kb:512 * kb + w], start=True, stop=True),
                                waits=[("scl", prev_s), ("vec", prev_v)], inc="tns")
                        smm = pb.now("tns")
                        pb.emit("vector", lambda e, klen=klen: e.tensor_tensor(
                            pbig[:, klen - 128:klen], pbig[:, klen - 128:klen], mask_d[:],
                            op=ALU.add), waits=[("tns", smm)], inc="vec")
                        vmask = pb.now("vec")
                        pb.emit("vector", lambda e, klen=klen: e.tensor_reduce(
                            col_tmp[:, 8:9], pbig[:, 0:klen], axis=AX.X, op=ALU.max,
                            negate=True), waits=[("vec", vmask)], inc="vec")
                        vmax = pb.now("vec")
                        pb.emit("vector", lambda e: e.tensor_scalar(
                            col_tmp[:, 9:10], col_tmp[:, 8:9], SQRT_D_INV, None, op0=ALU.mult),
                            waits=[("vec", vmax)], inc="vec")
                        vbias = pb.now("vec")
                        pb.emit("scalar", lambda e, klen=klen, ql=ql: e.activation(
                            Pq[:, T * ql:T * ql + klen], pbig[:, 0:klen],
                            AF.Exp, bias=col_tmp[:, 9:10], scale=SQRT_D_INV,
                            accum_out=sumc[:, ql:ql + 1]),
                            waits=[("vec", vbias)], inc="scl")
                        exp_s = pb.now("scl")
                    pb.emit("vector", lambda e: e.reciprocal(sumc[:, 0:4], sumc[:, 0:4]),
                            waits=[("scl", exp_s), ("tns", prev_av)], inc="vec")
                    vrc = pb.now("vec")
                    for ql in range(4):
                        pb.emit("vector", lambda e, ql=ql: e.tensor_scalar(
                            diag[:, 128 * ql:128 * ql + 128], ident_b[:], sumc[:, ql:ql + 1],
                            None, op0=ALU.mult), waits=[("vec", vrc)], inc="vec")
                    vdg = pb.now("vec")
                    av_last = 0
                    pt_ev = {0: 0, 1: 0}
                    nkt = 4 * (gq + 1)
                    for kt in range(nkt):
                        q0 = max(0, kt - 4 * gq)
                        pt_ps = pp[2 + (kt % 2)]
                        ptsb = PTb[:, 512 * (kt % 2):512 * (kt % 2) + 512]
                        for ql in range(q0, 4):
                            pb.emit("tensor", lambda e, kt=kt, ql=ql, pt_ps=pt_ps: e.matmul(
                                pt_ps[:, 128 * ql:128 * ql + 128],
                                Pq[:, T * ql + 128 * kt:T * ql + 128 * kt + 128],
                                diag[:, 128 * ql:128 * ql + 128], start=True, stop=True),
                                waits=[("vec", vdg), ("scl", pt_ev[kt % 2])], inc="tns")
                        tpt = pb.now("tns")
                        pb.emit("scalar", lambda e, pt_ps=pt_ps, ptsb=ptsb, q0=q0: e.copy(
                            ptsb[:, 128 * q0:512], pt_ps[:, 128 * q0:512]),
                            waits=[("tns", tpt)], inc="scl")
                        pt_ev[kt % 2] = pb.now("scl")
                        pb.emit("tensor", lambda e, kt=kt, ptsb=ptsb, q0=q0, nkt=nkt: e.matmul(
                            pp[4][:, 128 * q0:512], vtm[:, 128 * kt:128 * kt + 128],
                            ptsb[:, 128 * q0:512],
                            start=(kt == 0), stop=(kt == nkt - 1)),
                            waits=[("scl", max(pt_ev[kt % 2], o_evac))], inc="tns")
                        av_last = pb.now("tns")
                    prev_av = av_last
                    pb.emit("scalar", lambda e, gq=gq, j=j: e.copy(
                        Ob[j][:, 512 * gq:512 * gq + 512], pp[4]),
                        waits=[("tns", av_last)], inc="scl")
                    o_evac = pb.now("scl")
        if STAGE in ("qk0", "qraw"):
            pb.flush("ph1dbg")
            finish()
            return
        with ExitStack() as c1b:
            wo_f = sbl(c1b, "wo_f", 2048, F32)
            wo_b = sbl(c1b, "wo_b", 4 * 2048, BF16)
            s0sb = sbl(c1b, "s0sb", 2 * 2048, F32)
            for hch in range(4):
                wld = pb.dma("sync", wo_f[:],
                             g["woT"][128 * hch:128 * hch + 128, :].bitcast(F32),
                             waits=[("vec", pb.now("vec"))], inc="dls")
                pb.emit("vector", lambda e, hch=hch: e.tensor_copy(
                    wo_b[:, 2048 * hch:2048 * hch + 2048], wo_f[:]),
                    waits=[("dls", wld)], inc="vec")
            wcast = pb.now("vec")
            st_dma = {0: 0, 1: 0}
            prev_ev = pb.now("scl")
            for tt in range(32):
                for nb_ in range(4):
                    for hch in range(4):
                        pb.emit("tensor", lambda e, tt=tt, nb_=nb_, hch=hch: e.matmul(
                            pp[nb_], Ob[hch][:, 128 * tt:128 * tt + 128],
                            wo_b[:, 2048 * hch + 512 * nb_:2048 * hch + 512 * nb_ + 512],
                            start=(hch == 0), stop=(hch == 3)),
                            waits=[("vec", wcast), ("scl", prev_ev)],
                            inc="tns")
                mmd = pb.now("tns")
                dst = s0sb[:, (tt % 2) * 2048:(tt % 2) * 2048 + 2048]
                for nb_ in range(4):
                    pb.emit("scalar", lambda e, nb_=nb_, dst=dst: e.copy(
                        dst[:, 512 * nb_:512 * nb_ + 512], pp[nb_]),
                        waits=[("tns", mmd), ("dst%d" % (tt % 2), st_dma[tt % 2])], inc="scl")
                sev = pb.now("scl")
                prev_ev = sev
                st_dma[tt % 2] = pb.dma("scalar", g["s0_part"][128 * tt:128 * tt + 128, :],
                                        dst, waits=[("scl", sev)], inc="dst%d" % (tt % 2))
            pb.emit("gpsimd", lambda e: e.collective_compute(
                "ReduceScatter", ALU.add, replica_groups=[[0, 1, 2, 3], [4, 5, 6, 7]],
                ins=[g["s0_part"].opt()], outs=[g["s0_out"].opt()]),
                waits=[("dst0", pb.now("dst0")), ("dst1", pb.now("dst1"))], inc="cc")
            cc_rs0 = pb.now("cc")
        pb.flush("ph1")

    if STAGE == "s0":
        pb.wait_only("scalar", [("cc", cc_rs0)])
        for tt in range(8):
            pb.dma("scalar", g["out"][128 * tt:128 * tt + 128, :],
                   g["s0_out"][128 * tt:128 * tt + 128, :], inc="dsc")
        finish()
        return

    ctx01.close()
    with ExitStack() as c2:
        xnl = sbl(c2, "xnl", 16 * 512, F32R)
        gsb = sbl(c2, "gsb", 16 * 8, F32R)
        lgsb = sbl(c2, "lgsb", 32, F32)
        ptmp = sbl(c2, "ptmp", 32, F32)
        combtm = sbl(c2, "combtm", 32, F32)
        combfm = sbl(c2, "combfm", 1024, F32R, rows=8)
        crep = sbl(c2, "crep", 512, F32R)
        busb = sbl(c2, "busb", 2048, F32R, rows=8)
        bdsb = sbl(c2, "bdsb", 32, F32)
        bsdsb = sbl(c2, "bsdsb", 4, F32)
        bsurep = sbl(c2, "bsurep", 2048, F32)
        wdb = sbl(c2, "wdb", 2 * 512, F32R)
        wub = sbl(c2, "wub", 2 * 4 * 512, F32R)
        hb = [sbl(c2, "hb%d" % e_, 4 * 512, F32R) for e_ in range(9)]
        g0sb = sbl(c2, "g0sb", 4 * 512, F32)
        gld = pb.dma("sync", gsb[:].rearrange("p (c k) -> p c k", k=8),
                     g["gateT"].rearrange("(c p) k -> p c k", p=128).bitcast(F32R), inc="dls")
        gld = max(gld, pb.dma("sync", busb[:], g["bu"].bitcast(F32R), inc="dls"))
        gld = max(gld, pb.dma("sync", bdsb[:], g["bd"], inc="dls"))
        gld = max(gld, pb.dma("sync", bsdsb[:], g["bsd"], inc="dls"))
        gld = max(gld, pb.dma("sync", bsurep[:], g["bsu"].broadcast_to([128, 2048]), inc="dls"))
        prev_lg = 0
        prev_dn = pb.now("tns")
        g0_dma = {0: 0, 1: 0, 2: 0, 3: 0}
        for h2 in range(2):
            xl = 0
            for ec in range(16):
                xl = pb.dma("sync", xnl[:, 512 * ec:512 * ec + 512],
                            g["ag0_in"][128 * ec:128 * ec + 128, 512 * h2:512 * h2 + 512].bitcast(F32R),
                            waits=[("tns", prev_dn)], inc="dxn")
            xld = pb.now("dxn")
            for tl in range(4):
                for ec in range(16):
                    pb.emit("tensor", lambda e, tl=tl, ec=ec: e.matmul(
                        pp[5][:, 8 * tl:8 * tl + 8],
                        xnl[:, 512 * ec + 128 * tl:512 * ec + 128 * tl + 128],
                        gsb[:, 8 * ec:8 * ec + 8],
                        start=(ec == 0), stop=(ec == 15)),
                        waits=[("dxn", xld), ("dls", gld), ("scl", prev_lg)], inc="tns")
            gmm = pb.now("tns")
            pb.emit("scalar", lambda e: e.copy(lgsb[:, 0:32], pp[5][:, 0:32]),
                    waits=[("tns", gmm)], inc="scl")
            prev_lg = pb.now("scl")
            sev = prev_lg
            for tl in range(4):
                sl_ = slice(8 * tl, 8 * tl + 8)
                v0 = pb.now("vec")
                pb.emit("vector", lambda e, sl_=sl_: e.tensor_reduce(
                    col_tmp[:, 10:11], lgsb[:, sl_], axis=AX.X, op=ALU.max, negate=True),
                    waits=[("scl", sev), ("vec", v0)], inc="vec")
                vq = pb.now("vec")
                pb.emit("scalar", lambda e, sl_=sl_: e.activation(
                    ptmp[:, sl_], lgsb[:, sl_], AF.Exp, bias=col_tmp[:, 10:11],
                    accum_out=col_tmp[:, 11:12]),
                    waits=[("vec", vq)], inc="scl")
                sq = pb.now("scl")
                pb.emit("vector", lambda e: e.reciprocal(col_tmp[:, 11:12], col_tmp[:, 11:12]),
                        waits=[("scl", sq)], inc="vec")
                vq = pb.now("vec")
                pb.emit("vector", lambda e, sl_=sl_: e.tensor_scalar(
                    ptmp[:, sl_], ptmp[:, sl_], col_tmp[:, 11:12], None, op0=ALU.mult),
                    waits=[("vec", vq)], inc="vec")
                vq = pb.now("vec")
                pb.emit("vector", lambda e, sl_=sl_: e.tensor_reduce(
                    col_tmp[:, 12:13], ptmp[:, sl_], axis=AX.X, op=ALU.max),
                    waits=[("vec", vq)], inc="vec")
                vq = pb.now("vec")
                pb.emit("vector", lambda e, sl_=sl_: e.tensor_scalar(
                    combtm[:, sl_], ptmp[:, sl_], col_tmp[:, 12:13], None, op0=ALU.is_lt),
                    waits=[("vec", vq)], inc="vec")
                vq = pb.now("vec")
                pb.emit("vector", lambda e, sl_=sl_: e.tensor_tensor(
                    combtm[:, sl_], combtm[:, sl_], ptmp[:, sl_], op=ALU.mult),
                    waits=[("vec", vq)], inc="vec")
                vq = pb.now("vec")
                pb.emit("vector", lambda e, sl_=sl_: e.tensor_reduce(
                    col_tmp[:, 13:14], combtm[:, sl_], axis=AX.X, op=ALU.max),
                    waits=[("vec", vq)], inc="vec")
                vq = pb.now("vec")
                pb.emit("vector", lambda e: e.tensor_tensor(
                    col_tmp[:, 14:15], col_tmp[:, 12:13], col_tmp[:, 13:14], op=ALU.add),
                    waits=[("vec", vq)], inc="vec")
                vq = pb.now("vec")
                pb.emit("vector", lambda e: e.reciprocal(col_tmp[:, 14:15], col_tmp[:, 14:15]),
                        waits=[("vec", vq)], inc="vec")
                vq = pb.now("vec")
                pb.emit("vector", lambda e, sl_=sl_: e.tensor_scalar(
                    lgsb[:, sl_], ptmp[:, sl_], col_tmp[:, 13:14], None, op0=ALU.is_ge),
                    waits=[("vec", vq)], inc="vec")
                vq = pb.now("vec")
                pb.emit("vector", lambda e, sl_=sl_: e.tensor_tensor(
                    combtm[:, sl_], lgsb[:, sl_], ptmp[:, sl_], op=ALU.mult),
                    waits=[("vec", vq)], inc="vec")
                vq = pb.now("vec")
                pb.emit("vector", lambda e, sl_=sl_: e.tensor_scalar(
                    combtm[:, sl_], combtm[:, sl_], col_tmp[:, 14:15], None, op0=ALU.mult),
                    waits=[("vec", vq)], inc="vec")
            vcmb = pb.now("vec")
            ctr = 0
            for tl in range(4):
                pb.emit("tensor", lambda e, tl=tl: e.transpose(
                    pp[6][0:8, 0:128], combtm[:, 8 * tl:8 * tl + 8], ident_f[:]),
                    waits=[("vec", vcmb), ("scl", ctr)], inc="tns")
                tcm = pb.now("tns")
                pb.emit("scalar", lambda e, tl=tl, h2=h2: e.copy(
                    combfm[:, 512 * h2 + 128 * tl:512 * h2 + 128 * tl + 128],
                    pp[6][0:8, 0:128].bitcast(F32R)),
                    waits=[("tns", tcm)], inc="scl")
                ctr = pb.now("scl")
            pb.dma("scalar", g["comb_d"][:, 512 * h2:512 * h2 + 512],
                   combfm[:, 512 * h2:512 * h2 + 512].bitcast(F32),
                   waits=[("scl", ctr)], inc="dsc")
            cdma = pb.now("dsc")
            hready_v = 0
            hevac = 0
            crl = cdma
            dn_gate = {0: pb.now("tns"), 1: pb.now("tns")}
            for e_ in range(9):
                if e_ < 8:
                    dsrc = g["wdT"].rearrange("x (c p) f -> x c p f", p=128)[e_]
                else:
                    dsrc = g["wsdT"].rearrange("(c p) f -> c p f", p=128)
                for ec in range(16):
                    slot = wdb[:, 512 * (ec % 2):512 * (ec % 2) + 512]
                    wl2 = pb.dma("sync", slot, dsrc[ec],
                                 waits=[("tns", dn_gate[ec % 2])], inc="dwd%d" % (ec % 2))
                    for fch in range(4):
                        pb.emit("tensor", lambda e, ec=ec, fch=fch, slot=slot: e.matmul(
                            pp[fch], slot[:, 128 * fch:128 * fch + 128],
                            xnl[:, 512 * ec:512 * ec + 512],
                            start=(ec == 0), stop=(ec == 15)),
                            waits=([("dwd0", wl2), ("dxn", xld), ("scl", hevac)]
                                   if ec == 0 else [("dwd%d" % (ec % 2), wl2)]),
                            inc="tns")
                    dn_gate[ec % 2] = pb.now("tns")
                dmm = pb.now("tns")
                if e_ < 8:
                    crl = pb.dma("scalar", crep[:],
                                 g["comb_d"][e_:e_ + 1, 512 * h2:512 * h2 + 512].broadcast_to([128, 512]).bitcast(F32R),
                                 waits=[("dsc", cdma), ("vec", hready_v)], inc="dcr")
                for fch in range(4):
                    bias_ap = (bdsb[:, 4 * e_ + fch:4 * e_ + fch + 1] if e_ < 8
                               else bsdsb[:, fch:fch + 1])
                    pb.emit("scalar", lambda e, e_=e_, fch=fch, bias_ap=bias_ap: e.activation(
                        hb[e_][:, 512 * fch:512 * fch + 512], pp[fch], AF.Gelu,
                        bias=bias_ap), waits=[("tns", dmm), ("dls", gld)], inc="scl")
                hevac = pb.now("scl")
                if e_ < 8:
                    vv = pb.now("vec")
                    for fch in range(4):
                        pb.emit("vector", lambda e, e_=e_, fch=fch: e.tensor_tensor(
                            hb[e_][:, 512 * fch:512 * fch + 512],
                            hb[e_][:, 512 * fch:512 * fch + 512], crep[:], op=ALU.mult),
                            waits=[("scl", hevac), ("dcr", crl), ("vec", vv)], inc="vec")
                    hready_v = pb.now("vec")
            hdone_v = pb.now("vec")
            hdone_s = pb.now("scl")
            for nb_ in range(4):
                started = set()
                ev_gate = pb.now("vec")
                wu_gate = {0: pb.now("tns"), 1: pb.now("tns")}
                for e_ in range(9):
                    if e_ < 8:
                        usrc = g["wuT"].rearrange("x (c p) f -> x c p f", p=128)[e_]
                    else:
                        usrc = g["wsuT"].rearrange("(c p) f -> c p f", p=128)
                    sl0 = (e_ % 2) * 4 * 512
                    wul = 0
                    for bc in range(4):
                        wul = pb.dma(
                            "sync", wub[:, sl0 + bc * 512:sl0 + bc * 512 + 512],
                            usrc[bc][:, 512 * nb_:512 * nb_ + 512],
                            waits=[("tns", wu_gate[e_ % 2])], inc="dwu%d" % (e_ % 2))
                    for tl in range(4):
                        for bc in range(4):
                            st_ = tl not in started
                            started.add(tl)
                            pb.emit("tensor", lambda e, e_=e_, bc=bc, tl=tl, st_=st_, sl0=sl0: e.matmul(
                                pp[tl], hb[e_][:, 512 * bc + 128 * tl:512 * bc + 128 * tl + 128],
                                wub[:, sl0 + bc * 512:sl0 + bc * 512 + 512],
                                start=st_, stop=False),
                                waits=[("dwu%d" % (e_ % 2), wul), ("vec", max(hdone_v, ev_gate)), ("scl", hdone_s)],
                                inc="tns")
                    wu_gate[e_ % 2] = pb.now("tns")
                for tl in range(4):
                    pb.emit("tensor", lambda e, tl=tl, nb_=nb_, h2=h2: e.matmul(
                        pp[tl], combfm[:, 512 * h2 + 128 * tl:512 * h2 + 128 * tl + 128],
                        busb[:, 512 * nb_:512 * nb_ + 512], start=False, stop=True),
                        waits=[("dls", gld)], inc="tns")
                upmm = pb.now("tns")
                for tl in range(4):
                    vv = pb.now("vec")
                    pb.emit("vector", lambda e, tl=tl, nb_=nb_: e.tensor_tensor(
                        g0sb[:, 512 * tl:512 * tl + 512], pp[tl],
                        bsurep[:, 512 * nb_:512 * nb_ + 512], op=ALU.add),
                        waits=[("tns", upmm), ("vec", vv), ("dg%d" % tl, g0_dma[tl]),
                               ("dls", gld)], inc="vec")
                    vup = pb.now("vec")
                    pb.emit("scalar", lambda e, tl=tl: e.activation(
                        g0sb[:, 512 * tl:512 * tl + 512],
                        g0sb[:, 512 * tl:512 * tl + 512], AF.Silu),
                        waits=[("vec", vup)], inc="scl")
                    srow_ = 512 * h2 + 128 * tl
                    sg = pb.now("scl")
                    g0_dma[tl] = pb.dma(
                        "scalar", g["g0_d"][srow_:srow_ + 128, 512 * nb_:512 * nb_ + 512],
                        g0sb[:, 512 * tl:512 * tl + 512],
                        waits=[("scl", sg)], inc="dg%d" % tl)
            prev_dn = pb.now("tns")
        pb.flush("ph2")

    if STAGE == "moe":
        pb.wait_only("scalar", [("dg%d" % t_, pb.now("dg%d" % t_)) for t_ in range(4)])
        for tt in range(8):
            pb.dma("scalar", g["out"][128 * tt:128 * tt + 128, :],
                   g["g0_d"][128 * tt:128 * tt + 128, :], inc="dsc")
        finish()
        return

    with ExitStack() as c3:
        xb_ = sbl(c3, "xb_", 2048, F32)
        s0b = sbl(c3, "s0b", 2048, F32)
        g0b = sbl(c3, "g0b", 2048, F32)
        x1b = sbl(c3, "x1b", 2048, F32)
        xn1r = sbl(c3, "xn1r", 2048, F32R)
        lnscr3 = sbl(c3, "lnscr3", 2048, F32)
        xn1h = sbl(c3, "xn1h", 2048, BF16)
        stg = sbl(c3, "stg", 4 * 128, F32R)
        stgh = sbl(c3, "stgh", 4 * 128, BF16)
        pb.wait_only("sync", [("cc", cc_rs0)])
        prev_v3 = pb.now("vec")
        ph3_ev = {0: 0, 1: 0}
        for tt in range(8):
            w0_ = [("vec", prev_v3)]
            l1 = pb.dma("sync", xb_[:], g["x_tok"][128 * tt:128 * tt + 128, :],
                        waits=w0_, inc="dls")
            l2 = pb.dma("sync", s0b[:], g["s0_out"][128 * tt:128 * tt + 128, :],
                        waits=w0_, inc="dls")
            l3 = pb.dma("sync", g0b[:], g["g0_d"][128 * tt:128 * tt + 128, :],
                        waits=w0_ + [("dg%d" % t_, pb.now("dg%d" % t_)) for t_ in range(4)], inc="dls")
            ld3 = pb.now("dls")
            v = pb.now("vec")
            pb.emit("vector", lambda e: e.tensor_tensor(s0b[:], s0b[:], g0b[:], op=ALU.mult),
                    waits=[("dls", ld3), ("vec", v)], inc="vec")
            v = pb.now("vec")
            pb.emit("vector", lambda e: e.tensor_tensor(x1b[:], xb_[:], s0b[:], op=ALU.add),
                    waits=[("vec", v), ("dsc", pb.now("dsc"))], inc="vec")
            v = pb.now("vec")
            pb.dma("scalar", g["x1_d"][128 * tt:128 * tt + 128, :], x1b[:],
                   waits=[("vec", v)], inc="dsc")
            if STAGE == "x1":
                pb.dma("scalar", g["out"][128 * tt:128 * tt + 128, :], x1b[:],
                       waits=[("vec", v)], inc="dsc")
            pb.emit("vector", lambda e: e.tensor_reduce(
                col_tmp[:, 0:1], x1b[:], axis=AX.X, op=ALU.add),
                waits=[("vec", v)], inc="vec")
            pb.emit("vector", lambda e: e.tensor_tensor(
                lnscr3[:, :], x1b[:], x1b[:], op=ALU.mult),
                waits=[("vec", v)], inc="vec")
            _vsq = pb.now("vec")
            pb.emit("vector", lambda e: e.tensor_reduce(
                col_tmp[:, 1:2], lnscr3[:, :], axis=AX.X, op=ALU.add),
                waits=[("vec", _vsq)], inc="vec")
            v = pb.now("vec"); s = pb.now("scl")
            pb.emit("vector", lambda e: e.tensor_scalar(
                col_tmp[:, 0:1], col_tmp[:, 0:1], 1.0 / E, None, op0=ALU.mult),
                waits=[("vec", v)], inc="vec")
            v = pb.now("vec")
            pb.emit("vector", lambda e: e.tensor_scalar(
                col_tmp[:, 2:3], col_tmp[:, 0:1], col_tmp[:, 0:1], None, op0=ALU.mult),
                waits=[("vec", v)], inc="vec")
            v = pb.now("vec")
            pb.emit("vector", lambda e: e.scalar_tensor_tensor(
                col_tmp[:, 1:2], col_tmp[:, 1:2], 1.0 / E, col_tmp[:, 2:3],
                op0=ALU.mult, op1=ALU.subtract),
                waits=[("vec", v), ("scl", s)], inc="vec")
            v = pb.now("vec")
            pb.emit("scalar", lambda e: e.activation(
                col_tmp[:, 3:4], col_tmp[:, 1:2], AF.Ln, bias=ccol(LN_EPS)),
                waits=[("vec", v)], inc="scl")
            s = pb.now("scl")
            pb.emit("scalar", lambda e: e.activation(
                col_tmp[:, 3:4], col_tmp[:, 3:4], AF.Exp, scale=0.5),
                waits=[("scl", s)], inc="scl")
            s = pb.now("scl")
            pb.emit("vector", lambda e: e.reciprocal(col_tmp[:, 3:4], col_tmp[:, 3:4]),
                    waits=[("scl", s)], inc="vec")
            v = pb.now("vec")
            pb.emit("vector", lambda e: e.tensor_scalar(
                xn1r[:, :], x1b[:], col_tmp[:, 0:1], col_tmp[:, 3:4],
                op0=ALU.subtract, op1=ALU.mult),
                waits=[("vec", v)], inc="vec")
            pb.emit("vector", lambda e: e.tensor_scalar(
                xn1h[:, :], x1b[:], col_tmp[:, 0:1], col_tmp[:, 3:4],
                op0=ALU.subtract, op1=ALU.mult),
                waits=[("vec", v)], inc="vec")
            v = pb.now("vec")
            prev_v3 = v
            for ec in range(16):
                pA = pp[ec % 2]
                pB = pp[2 + ec % 2]
                dsl = pb.now("ds%d" % (ec % 2))
                pb.emit("tensor", lambda e, ec=ec, pA=pA: e.transpose(
                    pA[:, 0:128].bitcast(F32R), xn1r[:, 128 * ec:128 * ec + 128], ident_r[:]),
                    waits=[("vec", v), ("scl", ph3_ev[ec % 2])], inc="tns")
                t1 = pb.now("tns")
                pb.emit("tensor", lambda e, ec=ec, pB=pB: e.transpose(
                    pB[:, 0:64].bitcast(BF16), xn1h[:, 128 * ec:128 * ec + 128], ident_b[:]),
                    waits=[("vec", v), ("scl", ph3_ev[ec % 2])], inc="tns")
                t2 = pb.now("tns")
                pb.emit("scalar", lambda e, ec=ec, pA=pA: e.copy(
                    stg[:, 128 * (ec % 2):128 * (ec % 2) + 128], pA[:, 0:128].bitcast(F32R)),
                    waits=[("tns", t1), ("ds%d" % (ec % 2), dsl)], inc="scl")
                s1_ = pb.now("scl")
                pb.emit("scalar", lambda e, ec=ec, pB=pB: e.copy(
                    stgh[:, 128 * (ec % 2):128 * (ec % 2) + 128], pB[:, 0:64].bitcast(BF16)),
                    waits=[("tns", t2), ("ds%d" % (ec % 2), dsl)], inc="scl")
                s2_ = pb.now("scl")
                ph3_ev[ec % 2] = s2_
                pb.dma("scalar", g["ag1r_in"][128 * ec:128 * ec + 128, 128 * tt:128 * tt + 128],
                       stg[:, 128 * (ec % 2):128 * (ec % 2) + 128].bitcast(F32),
                       waits=[("scl", s1_)], inc="ds%d" % (ec % 2))
                pb.dma("scalar", g["ag1b_in"][128 * ec:128 * ec + 128, 128 * tt:128 * tt + 128],
                       stgh[:, 128 * (ec % 2):128 * (ec % 2) + 128],
                       waits=[("scl", s2_)], inc="ds%d" % (ec % 2))
                prev_tile = None
        pb.emit("gpsimd", lambda e: e.collective_compute(
            "AllGather", ALU.bypass, replica_groups=[[0, 1, 2, 3, 4, 5, 6, 7]],
            ins=[g["ag1b_in"].opt()], outs=[g["ag1b_out"].opt()]),
            waits=[("ds0", pb.now("ds0")), ("ds1", pb.now("ds1"))], inc="cc")
        cc_ag1b = pb.now("cc")
        pb.emit("gpsimd", lambda e: e.collective_compute(
            "AllGather", ALU.bypass, replica_groups=[[0, 1, 2, 3], [4, 5, 6, 7]],
            ins=[g["ag1r_in"].opt()], outs=[g["ag1r_out"].opt()]),
            inc="cc")
        cc_ag1r = pb.now("cc")
        pb.flush("ph3")

    if STAGE == "x1":
        finish()
        return

    NCH = ni // 128
    with ExitStack() as c4:
        shQ = sbl(c4, "shQ", 2048, I16)
        wgu = sbl(c4, "wgu", 16 * 2 * ni, BF16)
        wdn = sbl(c4, "wdn", NCH * 2048, BF16)
        pkb = sbl(c4, "pkb", 2 * 2048, U16)
        dq_a = sbl(c4, "dq_a", 2048, I16)
        dq_b = sbl(c4, "dq_b", 2048, I16)
        dq_c = sbl(c4, "dq_c", 2048, I16)
        dq_d = sbl(c4, "dq_d", 2048, I16)
        scrp = sbl(c4, "scrp", 2 * 2048, F32)
        sc16 = sbl(c4, "sc16", 2048, mybir.dt.float16, rows=16)
        sc32 = sbl(c4, "sc32", 2048, F32, rows=16)
        xnb = sbl(c4, "xnb", 2 * 16 * 512, BF16)
        gub = sbl(c4, "gub", 12 * 512, BF16)
        hact = sbl(c4, "hact", NCH * 512, BF16)
        ysb = sbl(c4, "ysb", 2 * 2048, BF16)
        pb.emit("gpsimd", lambda e: e.iota(shQ[:], [[0, 2048]], channel_multiplier=1),
                inc="gp")
        gsh = pb.now("gp")
        pb.emit("vector", lambda e: e.tensor_scalar(
            shQ[:], shQ[:], 3, 2, op0=ALU.bitwise_and, op1=ALU.logical_shift_left),
            waits=[("gp", gsh)], inc="vec")
        vsh = pb.now("vec")
        for nm, dd, w_, rr in (("gsc", "scg_d", ni, 16), ("usc", "scu_d", ni, 16),
                               ("dsc", "scd_d", 2048, NCH)):
            sld = pb.dma("sync", sc16[0:rr, 0:w_], g[nm],
                         waits=[("vec", pb.now("vec")), ("dsc", pb.now("dsc"))], inc="dls")
            pb.emit("vector", lambda e, w_=w_, rr=rr: e.tensor_copy(
                sc32[0:rr, 0:w_], sc16[0:rr, 0:w_]),
                waits=[("dls", sld), ("scl", pb.now("scl")), ("vec", pb.now("vec"))], inc="vec")
            vv4 = pb.now("vec")
            pb.dma("scalar", g[dd], sc32[0:rr, 0:w_], waits=[("vec", vv4)], inc="dsc")
        scd_done = pb.now("dsc")

        def dequant(dst, width, pk_src, sc_rows, nch_, gate=None):
            dec_prev = 0
            for c_ in range(nch_):
                slot = (c_ % 2)
                pkt = pkb[:, 2048 * slot:2048 * slot + width]
                srt = scrp[:, 2048 * slot:2048 * slot + width]
                lds = []
                for sidx, (srcap, off, wid) in enumerate(pk_src(c_)):
                    lds.append(pb.dma("sync", pkt[:, off:off + wid], srcap,
                                      waits=[("vec", dec_prev)], inc="dk%d" % slot))
                for sidx, (srcap, off, wid) in enumerate(sc_rows(c_)):
                    lds.append(pb.dma("scalar", srt[:, off:off + wid], srcap,
                                      waits=[("vec", dec_prev), ("dsc", scd_done)],
                                      inc="dk%d" % slot))
                ldv = pb.now("dk%d" % slot)
                pki = pkt.bitcast(I16)
                v_ = pb.now("vec")
                pb.emit("vector", lambda e, pki=pki, width=width: e.tensor_tensor(
                    dq_a[:, 0:width], pki, shQ[:, 0:width], op=ALU.logical_shift_right),
                    waits=[("dk%d" % slot, ldv), ("vec", v_)], inc="vec")
                v_ = pb.now("vec")
                pb.emit("vector", lambda e, width=width: e.tensor_scalar(
                    dq_b[:, 0:width], dq_a[:, 0:width], 7, None, op0=ALU.bitwise_and),
                    waits=[("vec", v_)], inc="vec")
                pb.emit("vector", lambda e, width=width: e.tensor_scalar(
                    dq_c[:, 0:width], dq_a[:, 0:width], 8, 12, op0=ALU.bitwise_and,
                    op1=ALU.logical_shift_left), waits=[("vec", v_)], inc="vec")
                v_ = pb.now("vec")
                pb.emit("vector", lambda e, width=width: e.tensor_scalar(
                    dq_a[:, 0:width], dq_b[:, 0:width], 64, 16128, op0=ALU.mult,
                    op1=ALU.add), waits=[("vec", v_)], inc="vec")
                v_ = pb.now("vec")
                pb.emit("vector", lambda e, width=width: e.tensor_tensor(
                    dq_a[:, 0:width], dq_a[:, 0:width], dq_c[:, 0:width],
                    op=ALU.bitwise_or), waits=[("vec", v_)], inc="vec")
                v_ = pb.now("vec")
                pb.emit("vector", lambda e, width=width: e.tensor_scalar(
                    dq_c[:, 0:width], dq_b[:, 0:width], 2, None, op0=ALU.is_lt),
                    waits=[("vec", v_)], inc="vec")
                pb.emit("vector", lambda e, width=width: e.tensor_scalar(
                    dq_d[:, 0:width], dq_b[:, 0:width], -0x3EC0, 0x3F00, op0=ALU.mult,
                    op1=ALU.add), waits=[("vec", v_)], inc="vec")
                v_ = pb.now("vec")
                pb.emit("vector", lambda e, width=width: e.tensor_tensor(
                    dq_d[:, 0:width], dq_d[:, 0:width], dq_c[:, 0:width], op=ALU.mult),
                    waits=[("vec", v_)], inc="vec")
                v_ = pb.now("vec")
                pb.emit("vector", lambda e, width=width: e.tensor_tensor(
                    dq_b[:, 0:width], dq_a[:, 0:width], dq_d[:, 0:width],
                    op=ALU.subtract), waits=[("vec", v_)], inc="vec")
                v_ = pb.now("vec")
                pb.emit("vector", lambda e, c_=c_, width=width, srt=srt: e.tensor_tensor(
                    dst[:, width * c_:width * c_ + width], dq_b[:, 0:width].bitcast(BF16),
                    srt[:, 0:width], op=ALU.mult),
                    waits=[("vec", v_)], inc="vec")
                dec_prev = pb.now("vec")
            return dec_prev

        dq1 = dequant(
            wgu, 2 * ni,
            pk_src=lambda c_: [(g["gpk"][128 * c_:128 * c_ + 128, :], 0, ni),
                               (g["upk"][128 * c_:128 * c_ + 128, :], ni, ni)],
            sc_rows=lambda c_: [(g["scg_d"][c_:c_ + 1, :].broadcast_to([128, ni]), 0, ni),
                                (g["scu_d"][c_:c_ + 1, :].broadcast_to([128, ni]), ni, ni)],
            nch_=16)
        dq2 = dequant(
            wdn, 2048,
            pk_src=lambda c_: [(g["dpk"][128 * c_:128 * c_ + 128, :], 0, 2048)],
            sc_rows=lambda c_: [(g["scd_d"][c_:c_ + 1, :].broadcast_to([128, 2048]), 0, 2048)],
            nch_=NCH)
        pb.wait_only("tensor", [("cc", cc_ag1b)])
        pb.wait_only("sync", [("cc", cc_ag1b)])
        yd = {0: 0, 1: 0}
        xn_mm = {0: 0, 1: 0}
        ev_gu = 0
        for tb in range(16):
            r_ = tb // 2
            co = 512 * (tb % 2)
            slot = tb % 2
            xll = 0
            for ec in range(16):
                xll = pb.dma("sync", xnb[:, 8192 * slot + 512 * ec:8192 * slot + 512 * ec + 512],
                             g["ag1b_out"][2048 * r_ + 128 * ec:2048 * r_ + 128 * ec + 128,
                                           co:co + 512],
                             waits=[("tns", xn_mm[slot])], inc="dx%d" % slot)
            hprev = pb.now("vec")
            for oc in range(2 * NCH):
                bank = pp[4 + (oc % 2)]
                for ec in range(16):
                    pb.emit("tensor", lambda e, oc=oc, ec=ec, slot=slot, bank=bank: e.matmul(
                        bank, wgu[:, 2 * ni * ec + 128 * oc:2 * ni * ec + 128 * oc + 128],
                        xnb[:, 8192 * slot + 512 * ec:8192 * slot + 512 * ec + 512],
                        start=(ec == 0), stop=(ec == 15)),
                        waits=[("dx%d" % slot, xll), ("vec", dq1), ("scl", ev_gu)],
                        inc="tns")
                gmm4 = pb.now("tns")
                if oc < NCH:
                    pb.emit("scalar", lambda e, oc=oc, bank=bank: e.activation(
                        gub[:, 512 * oc:512 * oc + 512], bank, AF.Silu),
                        waits=[("tns", gmm4), ("vec", hprev)], inc="scl")
                else:
                    pb.emit("scalar", lambda e, oc=oc, bank=bank: e.copy(
                        gub[:, 512 * oc:512 * oc + 512], bank),
                        waits=[("tns", gmm4), ("vec", hprev)], inc="scl")
                ev_gu = pb.now("scl")
            xn_mm[slot] = pb.now("tns")
            for ic in range(NCH):
                pb.emit("vector", lambda e, ic=ic: e.tensor_tensor(
                    hact[:, 512 * ic:512 * ic + 512], gub[:, 512 * ic:512 * ic + 512],
                    gub[:, 512 * (NCH + ic):512 * (NCH + ic) + 512], op=ALU.mult),
                    waits=[("scl", ev_gu), ("tns", pb.now("tns"))], inc="vec")
            hmul = pb.now("vec")
            ev_dn = pb.now("scl")
            for tl in range(4):
                for nb_ in range(4):
                    for ic in range(NCH):
                        pb.emit("tensor", lambda e, tl=tl, nb_=nb_, ic=ic: e.matmul(
                            pp[nb_], hact[:, 512 * ic + 128 * tl:512 * ic + 128 * tl + 128],
                            wdn[:, 2048 * ic + 512 * nb_:2048 * ic + 512 * nb_ + 512],
                            start=(ic == 0), stop=(ic == NCH - 1)),
                            waits=[("vec", max(hmul, dq2)), ("dy%d" % (tl % 2), yd[tl % 2]),
                                   ("scl", ev_dn)],
                            inc="tns")
                dmm4 = pb.now("tns")
                dsty = ysb[:, 2048 * (tl % 2):2048 * (tl % 2) + 2048]
                for nb_ in range(4):
                    pb.emit("scalar", lambda e, nb_=nb_, dsty=dsty: e.copy(
                        dsty[:, 512 * nb_:512 * nb_ + 512], pp[nb_]),
                        waits=[("tns", dmm4), ("dy%d" % (tl % 2), yd[tl % 2])], inc="scl")
                yev = pb.now("scl")
                ev_dn = yev
                yd[tl % 2] = pb.dma("scalar",
                                    g["y1_part"][512 * tb + 128 * tl:512 * tb + 128 * tl + 128, :],
                                    dsty, waits=[("scl", yev)], inc="dy%d" % (tl % 2))
        pb.emit("gpsimd", lambda e: e.collective_compute(
            "ReduceScatter", ALU.add, replica_groups=[[0, 1, 2, 3, 4, 5, 6, 7]],
            ins=[g["y1_part"].opt()], outs=[g["y1_out"].opt()]),
            waits=[("dy0", pb.now("dy0")), ("dy1", pb.now("dy1"))], inc="cc")
        cc_rs_y = pb.now("cc")
        pb.flush("ph4")

    if STAGE == "ffn":
        ysbh = sb("ysbh", 2048, BF16)
        ysbf = sb("ysbf", 2048, F32)
        pb.wait_only("sync", [("cc", cc_rs_y)])
        for tt in range(8):
            vv9 = pb.now("vec")
            ld9 = pb.dma("sync", ysbh[:], g["y1_out"][128 * tt:128 * tt + 128, :],
                         waits=[("vec", vv9)], inc="dls")
            pb.emit("vector", lambda e: e.tensor_copy(ysbf[:], ysbh[:]),
                    waits=[("dls", ld9), ("dsc", pb.now("dsc"))], inc="vec")
            vv9 = pb.now("vec")
            pb.dma("scalar", g["out"][128 * tt:128 * tt + 128, :], ysbf[:],
                   waits=[("vec", vv9)], inc="dsc")
        finish()
        return


    with ExitStack() as c5a:
      O1 = [sbl(c5a, "O1%d" % j, T, F32R) for j in range(4)]
      smalls = sbl(c5a, "smalls", 32 * 20, F32)
      grep_ = sbl(c5a, "grep_", 128, F32)
      brep_ = sbl(c5a, "brep_", 128, F32)
      with ExitStack() as c5:
        wsmb = sbl(c5, "wsmb", 16 * 20, F32R)
        bsmr = sbl(c5, "bsmr", 20, F32)
        xfm = sbl(c5, "xfm", 2 * 2048, F32R)
        smt = sbl(c5, "smt", 32 * 20, F32)
        sqs = sbl(c5, "sqs", 768, F32)
        pb.wait_only("tensor", [("cc", cc_ag1r), ("cc", cc_ag1b)])
        pb.wait_only("sync", [("cc", cc_ag1r), ("cc", cc_ag1b)])
        wsl = pb.dma("sync", wsmb[:].rearrange("p (c k) -> p c k", k=20),
                     g["wsm"].rearrange("(c p) k -> p c k", p=128), inc="dls")
        wsl = max(wsl, pb.dma("sync", bsmr[:], g["bsm"].broadcast_to([128, 20]), inc="dls"))
        wsl = max(wsl, pb.dma("sync", grep_[:], g["lng"].broadcast_to([128, 128]), inc="dls"))
        wsl = max(wsl, pb.dma("sync", brep_[:], g["lnb"].broadcast_to([128, 128]), inc="dls"))
        ag1src = g["ag1r_out"].rearrange("(r c p) t -> r p c t", p=128, c=16)
        xmm = {0: 0, 1: 0}
        ev_s = pb.now("vec")
        scl_s = pb.now("scl")
        smdma = 0
        for tt in range(32):
            slot = tt % 2
            xl5 = pb.dma("sync",
                         xfm[:, 2048 * slot:2048 * slot + 2048].rearrange(
                             "p (c k) -> p c k", k=128),
                         ag1src[tt // 8][:, :, 128 * (tt % 8):128 * (tt % 8) + 128].bitcast(F32R),
                         waits=[("tns", xmm[slot])], inc="df%d" % slot)
            for ec in range(16):
                pb.emit("tensor", lambda e, ec=ec, slot=slot: e.matmul(
                    pbig[:, 1024:1044],
                    xfm[:, 2048 * slot + 128 * ec:2048 * slot + 128 * ec + 128],
                    wsmb[:, 20 * ec:20 * ec + 20],
                    start=(ec == 0), stop=(ec == 15)),
                    waits=[("df%d" % slot, xl5), ("dls", wsl),
                           ("vec", ev_s), ("scl", scl_s)], inc="tns")
            xmm[slot] = pb.now("tns")
            qmm = pb.now("tns")
            sm = smt[:, 20 * tt:20 * tt + 20]
            pb.emit("vector", lambda e, sm=sm: e.tensor_tensor(
                sm, pbig[:, 1024:1044], bsmr[:], op=ALU.add),
                waits=[("tns", qmm), ("vec", pb.now("vec")), ("dls", wsl)], inc="vec")
            v5b = pb.now("vec")
            ev_s = v5b
            pb.emit("vector", lambda e, sm=sm: e.tensor_scalar(
                sqs[:, 24:36], sm[:, 0:12], -1.0, 0.0, op0=ALU.mult, op1=ALU.max),
                waits=[("vec", v5b), ("scl", pb.now("scl"))], inc="vec")
            _sv = pb.now("vec")
            pb.emit("vector", lambda e, sm=sm: e.tensor_scalar(
                sqs[:, 0:12], sm[:, 0:12], 0.0, None, op0=ALU.max),
                waits=[("vec", _sv)], inc="vec")
            _sv = pb.now("vec")
            pb.emit("vector", lambda e: e.tensor_tensor(
                sqs[:, 0:12], sqs[:, 0:12], sqs[:, 24:36], op=ALU.add),
                waits=[("vec", _sv)], inc="vec")
            s5v = pb.now("vec")
            pb.emit("scalar", lambda e: e.activation(
                sqs[:, 0:12], sqs[:, 0:12], AF.Exp, scale=-1.0),
                waits=[("vec", s5v)], inc="scl")
            s5 = pb.now("scl")
            pb.emit("scalar", lambda e: e.activation(
                sqs[:, 0:12], sqs[:, 0:12], AF.Ln, bias=ccol(1.0)),
                waits=[("scl", s5)], inc="scl")
            s5 = pb.now("scl")
            pb.emit("vector", lambda e, sm=sm: e.tensor_scalar(
                sqs[:, 12:24], sm[:, 0:12], 0.0, None, op0=ALU.max),
                waits=[("vec", v5b), ("scl", s5)], inc="vec")
            v5c = pb.now("vec")
            pb.emit("vector", lambda e, sm=sm: e.tensor_tensor(
                sm[:, 0:12], sqs[:, 0:12], sqs[:, 12:24], op=ALU.add),
                waits=[("vec", v5c), ("scl", s5)], inc="vec")
            vlr = pb.now("vec")
            pb.emit("scalar", lambda e, sm=sm: e.activation(
                sm[:, 12:16], sm[:, 12:16], AF.Sigmoid),
                waits=[("vec", v5b)], inc="scl")
            pb.emit("scalar", lambda e, sm=sm: e.activation(
                sm[:, 16:20], sm[:, 16:20], AF.Silu),
                waits=[("vec", v5b)], inc="scl")
            scl_s = pb.now("scl")
            smdma = pb.dma("scalar", g["smalls_d"][128 * tt:128 * tt + 128, :], sm,
                           waits=[("vec", vlr), ("scl", scl_s)], inc="dsm")
        smload = pb.dma("sync", smalls[:].rearrange("p (c h k) -> p c h k", h=16, k=20),
                        g["smalls_d"].rearrange("(c m h) k -> m c h k", m=128, h=16),
                        waits=[("dsm", smdma)], inc="dls")
        xfmb = sbl(c5, "xfmb", 8 * 2048, BF16)
        wchb = sbl(c5, "wchb", 2 * 16 * 512, BF16)
        qvs = sbl(c5, "qvs", 2 * 512, BF16)
        ag1bsrc = g["ag1b_in"].rearrange("(c p) t -> p c t", p=128)
        xbl = 0
        for t8 in range(8):
            xbl = pb.dma("sync",
                         xfmb[:, 2048 * t8:2048 * t8 + 2048].rearrange(
                             "p (c k) -> p c k", k=128),
                         ag1bsrc[:, :, 128 * t8:128 * t8 + 128], inc="dls")
        wqsrc = g["wqkv"].bitcast(BF16).rearrange("(c p) k -> p c k", p=128)
        wl2 = {0: 0, 1: 0}
        mmq = {0: 0, 1: 0}
        evq = {0: pb.now("scl"), 1: pb.now("scl")}
        vnq = {0: pb.now("vec"), 1: pb.now("vec")}
        qdma = {0: 0, 1: 0}
        for q in range(12):
            sl = q % 2
            wl2[sl] = pb.dma("sync",
                             wchb[:, 8192 * sl:8192 * sl + 8192].rearrange(
                                 "p (c k) -> p c k", k=512),
                             wqsrc[:, :, 512 * q:512 * q + 512],
                             waits=[("tns", mmq[sl])], inc="dw%d" % sl)
            for t8 in range(8):
                bk = t8 % 2
                for ec in range(16):
                    pb.emit("tensor", lambda e, t8=t8, ec=ec, sl=sl, bk=bk: e.matmul(
                        pbig[:, 512 * bk:512 * bk + 512],
                        xfmb[:, 2048 * t8 + 128 * ec:2048 * t8 + 128 * ec + 128],
                        wchb[:, 8192 * sl + 512 * ec:8192 * sl + 512 * ec + 512],
                        start=(ec == 0), stop=(ec == 15)),
                        waits=[("dw%d" % sl, wl2[sl]), ("dls", xbl),
                               ("scl", evq[bk]), ("vec", vnq[bk])], inc="tns")
                qm2 = pb.now("tns")
                qsl = qvs[:, 512 * bk:512 * bk + 512]
                pb.emit("scalar", lambda e, qsl=qsl, bk=bk: e.activation(
                    qsl, pbig[:, 512 * bk:512 * bk + 512], AF.Silu),
                    waits=[("tns", qm2), ("ds%d" % bk, qdma[bk]),
                           ("vec", vnq[bk])], inc="scl")
                evq[bk] = pb.now("scl")
                if q < 8:
                    for gg in range(4):
                        pb.emit("vector", lambda e, qsl=qsl, gg=gg: e.tensor_tensor(
                            sqs[:, 0:128], qsl[:, 128 * gg:128 * gg + 128],
                            qsl[:, 128 * gg:128 * gg + 128], op=ALU.mult),
                            waits=[("scl", evq[bk]), ("vec", pb.now("vec"))], inc="vec")
                        _vsq = pb.now("vec")
                        pb.emit("vector", lambda e, gg=gg: e.tensor_reduce(
                            col_tmp[:, 16 + gg:17 + gg], sqs[:, 0:128],
                            axis=AX.X, op=ALU.add),
                            waits=[("vec", _vsq)], inc="vec")
                    vq0 = pb.now("vec")
                    pb.emit("scalar", lambda e: e.activation(
                        col_tmp[:, 20:24], col_tmp[:, 16:20], AF.Ln),
                        waits=[("vec", vq0)], inc="scl")
                    sev5 = pb.now("scl")
                    pb.emit("scalar", lambda e: e.activation(
                        col_tmp[:, 20:24], col_tmp[:, 20:24], AF.Exp, scale=0.5),
                        waits=[("scl", sev5)], inc="scl")
                    sev5 = pb.now("scl")
                    pb.emit("vector", lambda e: e.tensor_scalar(
                        col_tmp[:, 20:24], col_tmp[:, 20:24], 1e-5, None, op0=ALU.add),
                        waits=[("scl", sev5), ("vec", pb.now("vec"))], inc="vec")
                    v5 = pb.now("vec")
                    pb.emit("vector", lambda e: e.reciprocal(
                        col_tmp[:, 20:24], col_tmp[:, 20:24]),
                        waits=[("vec", v5)], inc="vec")
                    v5 = pb.now("vec")
                    for gg in range(4):
                        pb.emit("vector", lambda e, qsl=qsl, gg=gg: e.tensor_scalar(
                            qsl[:, 128 * gg:128 * gg + 128],
                            qsl[:, 128 * gg:128 * gg + 128],
                            col_tmp[:, 20 + gg:21 + gg], None, op0=ALU.mult),
                            waits=[("vec", v5)], inc="vec")
                vnq[bk] = pb.now("vec")
                dst5 = (g["q1_d"] if q < 4 else g["k1_d"] if q < 8 else g["v1_d"])
                qdma[bk] = pb.dma("scalar",
                                  dst5[128 * t8:128 * t8 + 128,
                                       512 * (q % 4):512 * (q % 4) + 512],
                                  qsl, waits=[("vec", vnq[bk]), ("scl", evq[bk])],
                                  inc="ds%d" % bk)
            mmq[sl] = pb.now("tns")
        pb.flush("ph5a")
        if STAGE == "qkv1":
            qdbh = sbl(c5, "qdbh", 2048, BF16)
            qdbf = sbl(c5, "qdbf", 2048, F32)
            for tt in range(8):
                vv9 = pb.now("vec")
                ld9 = pb.dma("sync", qdbh[:], g["q1_d"][128 * tt:128 * tt + 128, :],
                             waits=[("vec", vv9), ("ds0", qdma[0]), ("ds1", qdma[1])],
                             inc="dls")
                pb.emit("vector", lambda e: e.tensor_copy(qdbf[:], qdbh[:]),
                        waits=[("dls", ld9), ("dsc", pb.now("dsc"))], inc="vec")
                vv9 = pb.now("vec")
                pb.dma("scalar", g["out"][128 * tt:128 * tt + 128, :], qdbf[:],
                       waits=[("vec", vv9)], inc="dsc")
            finish()
            return

      with ExitStack() as c5b:
        qtm = sbl(c5b, "qtm", 32 * 128, BF16)
        ktm = sbl(c5b, "ktm", 32 * 128, BF16)
        vtm5 = sbl(c5b, "vtm5", 32 * 128, BF16)
        qfm = sbl(c5b, "qfm", T, BF16)
        kfm = sbl(c5b, "kfm", T, BF16)
        vfm = sbl(c5b, "vfm", T, BF16)
        wma = sbl(c5b, "wma", 3 * 128, F32)
        wmb = sbl(c5b, "wmb", 3 * 128, BF16)
        wmbT = sbl(c5b, "wmbT", 3 * 128, BF16)
        wn0 = sbl(c5b, "wn0", 8, F32)
        gbq = sbl(c5b, "gbq", 2048, BF16)
        sgq = sbl(c5b, "sgq", 2048, BF16)
        sigq = sbl(c5b, "sigq", 2048, BF16)
        hbq = sbl(c5b, "hbq", 2048, BF16)
        dhq = sbl(c5b, "dhq", 2048, BF16)
        t1q = sbl(c5b, "t1q", 2048, BF16)
        t2q = sbl(c5b, "t2q", 2048, BF16)
        kpq = sbl(c5b, "kpq", 2 * 2048, BF16)
        ofm = sbl(c5b, "ofm", T, BF16)
        otm = sbl(c5b, "otm", 128, F32R)
        dwf = sbl(c5b, "dwf", 3 * 128, F32)
        nsA = sbl(c5b, "nsA", 128, F32)
        nsB = sbl(c5b, "nsB", 128, F32)
        nsT = sbl(c5b, "nsT", 128, F32)
        scr5 = sbl(c5b, "scr5", 512, F32)
        ghh = sbl(c5b, "ghh", 512, BF16)
        dbg_otm = sbl(c5b, "dbg_otm", 768, F32)

        def mm5(out_ap, lhs_ap, rhs_ap, waits, start=True, stop=True):
            pb.emit("tensor", lambda e, o=out_ap, l=lhs_ap, r=rhs_ap, st=start, sp=stop:
                    e.matmul(o, l, r, start=st, stop=sp), waits=waits, inc="tns")
            return pb.now("tns")

        def evac(dst, src, waits, act=None, accum=None):
            if act is None:
                pb.emit("scalar", lambda e, d=dst, s=src: e.copy(d, s), waits=waits, inc="scl")
            else:
                pb.emit("scalar", lambda e, d=dst, s=src, a=act, ac=accum: e.activation(
                    d, s, a, accum_out=ac) if ac is not None else e.activation(d, s, a),
                    waits=waits, inc="scl")
            return pb.now("scl")

        qsrc = g["q1_d"].rearrange("(s c m) (h d) -> s m c h d", s=4, c=2, h=16)
        ksrc = g["k1_d"].rearrange("(s c m) (h d) -> s m c h d", s=4, c=2, h=16)
        vsrc = g["v1_d"].rearrange("(s c m) (h d) -> s m c h d", s=4, c=2, h=16)
        pb.wait_only("vector", [("dls", smload)])
        pb.wait_only("tensor", [("dls", smload)])
        pb.wait_only("scalar", [("dls", smload)])
        for j in range(4):
            tdm = pb.dma("sync", qtm[:].rearrange("p (c h k) -> p c h k", h=16, k=128),
                         qsrc[j],
                         waits=[("tns", pb.now("tns")),
                                ("ds0", qdma[0]), ("ds1", qdma[1])], inc="dls")
            tdm = max(tdm, pb.dma("sync", ktm[:].rearrange("p (c h k) -> p c h k", h=16, k=128),
                                  ksrc[j], inc="dls"))
            tdm = max(tdm, pb.dma("sync", vtm5[:].rearrange("p (c h k) -> p c h k", h=16, k=128),
                                  vsrc[j], inc="dls"))
            wdm = pb.dma("sync", wma[:, 0:128], g["w0"][j], inc="dls")
            wdm = max(wdm, pb.dma("sync", wma[:, 128:256], g["w1"][j], inc="dls"))
            wdm = max(wdm, pb.dma("sync", wma[:, 256:384], g["w2"][j], inc="dls"))
            ev5 = {0: pb.now("scl"), 1: pb.now("scl")}
            for src_t, dst_t in ((qtm, qfm), (ktm, kfm), (vtm5, vfm)):
                for tt in range(32):
                    po = pp[6 + (tt % 2)][:, 0:64].bitcast(BF16)
                    pb.emit("tensor", lambda e, s=src_t, tt=tt, po=po: e.transpose(
                        po, s[:, 128 * tt:128 * tt + 128], ident_b[:]),
                        waits=[("dls", tdm), ("scl", ev5[tt % 2])], inc="tns")
                    t5 = pb.now("tns")
                    pb.emit("scalar", lambda e, d=dst_t, tt=tt, po=po: e.copy(
                        d[:, 128 * tt:128 * tt + 128], po),
                        waits=[("tns", t5)], inc="scl")
                    ev5[tt % 2] = pb.now("scl")
            tfm_done = pb.now("scl")
            v6 = pb.now("vec")
            pb.emit("vector", lambda e: e.tensor_copy(wmb[:], wma[:]),
                    waits=[("dls", wdm), ("vec", v6)], inc="vec")
            vwb = pb.now("vec")
            for wi in range(3):
                po = pp[5][:, 0:64].bitcast(BF16)
                pb.emit("tensor", lambda e, wi=wi, po=po: e.transpose(
                    po, wmb[:, 128 * wi:128 * wi + 128], ident_b[:]),
                    waits=[("vec", vwb), ("scl", pb.now("scl"))], inc="tns")
                t5 = pb.now("tns")
                pb.emit("scalar", lambda e, wi=wi, po=po: e.copy(
                    wmbT[:, 128 * wi:128 * wi + 128], po),
                    waits=[("tns", t5)], inc="scl")
            wt_done = pb.now("scl")
            vw6 = pb.now("vec")
            for wi in range(3):
                pb.emit("vector", lambda e, wi=wi: e.tensor_tensor(
                    scr5[:, 0:128], wma[:, 128 * wi:128 * wi + 128],
                    wma[:, 128 * wi:128 * wi + 128], op=ALU.mult),
                    waits=[("dls", wdm), ("scl", wt_done), ("vec", vw6)], inc="vec")
                _vsq = pb.now("vec")
                pb.emit("vector", lambda e, wi=wi: e.tensor_reduce(
                    wn0[:, wi:wi + 1], scr5[:, 0:128], axis=AX.X, op=ALU.add),
                    waits=[("vec", _vsq)], inc="vec")
                vw6 = pb.now("vec")
            pb.emit("scalar", lambda e: e.activation(wn0[:, 0:3], wn0[:, 0:3], AF.Ln),
                    waits=[("vec", vw6)], inc="scl")
            _s = pb.now("scl")
            pb.emit("scalar", lambda e: e.activation(wn0[:, 0:3], wn0[:, 0:3], AF.Exp, scale=0.5),
                    waits=[("scl", _s)], inc="scl")
            wn_done = pb.now("scl")

            def forward(chunk, wait0):
                last = wait0
                for blk in range(4):
                    qs = qfm[:, 2048 * chunk + 512 * blk:2048 * chunk + 512 * blk + 512]
                    m1_ = mm5(pp[0], wmbT[:, 0:128], qs,
                              [("scl", max(last, wn_done)), ("vec", pb.now("vec"))])
                    m2_ = mm5(pp[1], wmbT[:, 256:384], qs, [("scl", last)])
                    e1 = evac(ghh[:, 0:512], pp[0], [("tns", m2_)], act=AF.Silu)
                    e2 = evac(sgq[:, 0:512], pp[1], [("tns", m2_)])
                    v7 = pb.now("vec")
                    pb.emit("vector", lambda e: e.tensor_tensor(
                        ghh[:, 0:512], ghh[:, 0:512], sgq[:, 0:512], op=ALU.mult),
                        waits=[("scl", max(e1, e2)), ("vec", v7), ("tns", m2_)], inc="vec")
                    v7 = pb.now("vec")
                    m3_ = mm5(pp[2], wmbT[:, 128:256], ghh[:, 0:512],
                              [("vec", v7), ("scl", last)])
                    last = evac(ofm[:, 2048 * chunk + 512 * blk:2048 * chunk + 512 * blk + 512],
                                pp[2], [("tns", m3_)])
                return last

            f0 = forward(0, tfm_done)
            up_ev = f0
            for tt in range(16):
                sl5 = slice(128 * tt, 128 * tt + 128)
                m1_ = mm5(pp[4][:, 0:128], kfm[:, sl5], wmbT[:, 0:128],
                          [("scl", max(up_ev, wn_done))])
                m2_ = mm5(pp[4][:, 128:256], kfm[:, sl5], wmbT[:, 256:384], [])
                m3_ = mm5(pp[4][:, 256:384], vfm[:, sl5], wmb[:, 128:256], [])
                c1_ = evac(gbq[:, sl5], pp[4][:, 0:128], [("tns", m3_)])
                c2_ = evac(sgq[:, sl5], pp[4][:, 0:128], [("tns", m3_)], act=AF.Silu)
                c3_ = evac(sigq[:, sl5], pp[4][:, 0:128], [("tns", m3_)], act=AF.Sigmoid)
                c4_ = evac(hbq[:, sl5], pp[4][:, 128:256], [("tns", m3_)])
                up_ev = evac(dhq[:, sl5], pp[4][:, 256:384], [("tns", m3_)])
            v8 = pb.now("vec")

            def vop(fn, waits):
                pb.emit("vector", fn, waits=waits, inc="vec")
                return pb.now("vec")

            v8 = vop(lambda e: e.tensor_tensor(t1q[:], sgq[:], hbq[:], op=ALU.mult),
                     [("scl", up_ev), ("vec", v8)])
            v8 = vop(lambda e: e.tensor_tensor(t2q[:], dhq[:], sgq[:], op=ALU.mult),
                     [("vec", v8)])
            v8 = vop(lambda e: e.tensor_tensor(hbq[:], dhq[:], hbq[:], op=ALU.mult),
                     [("vec", v8)])
            v8 = vop(lambda e: e.tensor_scalar(sgq[:], sigq[:], -1.0, 1.0,
                                               op0=ALU.mult, op1=ALU.add), [("vec", v8)])
            v8 = vop(lambda e: e.tensor_tensor(gbq[:], gbq[:], sgq[:], op=ALU.mult),
                     [("vec", v8)])
            v8 = vop(lambda e: e.tensor_scalar(gbq[:], gbq[:], 1.0, None, op0=ALU.add),
                     [("vec", v8)])
            v8 = vop(lambda e: e.tensor_tensor(hbq[:], hbq[:], sigq[:], op=ALU.mult),
                     [("vec", v8)])
            v8 = vop(lambda e: e.tensor_tensor(hbq[:], hbq[:], gbq[:], op=ALU.mult),
                     [("vec", v8)])
            for tt in range(16):
                sl5 = slice(128 * tt, 128 * tt + 128)
                v8 = vop(lambda e, sl5=sl5, tt=tt, j=j: e.tensor_scalar(
                    t1q[:, sl5], t1q[:, sl5], smalls[:, 20 * tt + 4 + j:20 * tt + 5 + j],
                    None, op0=ALU.mult), [("vec", v8)])
                v8 = vop(lambda e, sl5=sl5, tt=tt, j=j: e.tensor_scalar(
                    kpq[:, sl5], ktm[:, sl5],
                    smalls[:, 20 * tt + j:20 * tt + 1 + j], None, op0=ALU.mult),
                    [("vec", v8)])
                v8 = vop(lambda e, sl5=sl5, tt=tt, j=j: e.tensor_scalar(
                    kpq[:, 2048 + 128 * tt:2048 + 128 * tt + 128], ktm[:, sl5],
                    smalls[:, 20 * tt + 8 + j:20 * tt + 9 + j], None, op0=ALU.mult),
                    [("vec", v8)])
            dwm = 0
            scl_dw = pb.now("scl")
            for tt in range(16):
                sl5 = slice(128 * tt, 128 * tt + 128)
                dwm = mm5(pp[5][:, 0:128], vtm5[:, sl5], t1q[:, sl5],
                          [("vec", v8), ("scl", scl_dw)], start=(tt == 0), stop=(tt == 15))
            for tt in range(16):
                sl5 = slice(128 * tt, 128 * tt + 128)
                dwm = mm5(pp[5][:, 128:256], hbq[:, sl5], kpq[:, sl5],
                          [("vec", v8)], start=(tt == 0), stop=(tt == 15))
            for tt in range(16):
                sl5 = slice(128 * tt, 128 * tt + 128)
                dwm = mm5(pp[5][:, 256:384], t2q[:, sl5],
                          kpq[:, 2048 + 128 * tt:2048 + 128 * tt + 128],
                          [("vec", v8)], start=(tt == 0), stop=(tt == 15))
            msum = 0
            for tt in range(16):
                msum = mm5(pbig[0:1, 3584:3585], smalls[:, 20 * tt + 12 + j:20 * tt + 13 + j],
                           ones_f[:, 0:1], [("vec", v8)], start=(tt == 0), stop=(tt == 15))
            e9 = evac(wn0[0:1, 3:4], pbig[0:1, 3584:3585], [("tns", msum)])
            mrep = mm5(pbig[:, 3712:3713], ones_f[0:1, 0:128], wn0[0:1, 3:4],
                       [("scl", e9)])
            v9 = pb.now("vec")
            pb.emit("vector", lambda e: e.tensor_scalar(
                wn0[:, 4:5], pbig[:, 3712:3713], 1.0 / 2048.0, 1.0,
                op0=ALU.mult, op1=ALU.add),
                waits=[("tns", mrep), ("vec", v9)], inc="vec")
            vmom = pb.now("vec")
            for wi in range(3):
                pb.emit("vector", lambda e, wi=wi: e.tensor_scalar(
                    dwf[:, 128 * wi:128 * wi + 128], pp[5][:, 128 * wi:128 * wi + 128],
                    wn0[:, 4:5], None, op0=ALU.mult),
                    waits=[("tns", dwm), ("vec", vmom)], inc="vec")
            vdw = pb.now("vec")
            if STAGE == "lct0" and j == 0:
                dbg_pre = sbl(c5b, "dbg_pre", 384, F32)
                pb.emit("vector", lambda e: e.tensor_copy(dbg_pre[:], dwf[:]),
                        waits=[("vec", vdw)], inc="vec")
                vdw = pb.now("vec")
            ns_done = []
            for wi in range(3):
                X = dwf[:, 128 * wi:128 * wi + 128]
                pb.emit("vector", lambda e, X=X: e.tensor_tensor(
                    scr5[:, 0:128], X, X, op=ALU.mult),
                    waits=[("vec", vdw), ("vec", pb.now("vec"))], inc="vec")
                _vsq = pb.now("vec")
                pb.emit("vector", lambda e: e.tensor_reduce(
                    wn0[:, 5:6], scr5[:, 0:128], axis=AX.X, op=ALU.add),
                    waits=[("vec", _vsq)], inc="vec")
                sEv = pb.now("vec")
                fro = mm5(pbig[:, 3840:3841], ones_f[:, 0:128], wn0[:, 5:6],
                          [("vec", sEv)])
                sF = evac(wn0[:, 6:7], pbig[:, 3840:3841], [("tns", fro)], act=AF.Ln)
                pb.emit("scalar", lambda e: e.activation(
                    wn0[:, 6:7], wn0[:, 6:7], AF.Exp, scale=0.5),
                    waits=[("scl", sF)], inc="scl")
                sF = pb.now("scl")
                v10 = pb.now("vec")
                pb.emit("vector", lambda e: e.tensor_scalar(
                    wn0[:, 6:7], wn0[:, 6:7], 1e-7, None, op0=ALU.add),
                    waits=[("scl", sF), ("vec", v10)], inc="vec")
                v10 = pb.now("vec")
                pb.emit("vector", lambda e: e.reciprocal(wn0[:, 6:7], wn0[:, 6:7]),
                        waits=[("vec", v10)], inc="vec")
                v10 = pb.now("vec")
                pb.emit("vector", lambda e, X=X: e.tensor_scalar(
                    X, X, wn0[:, 6:7], None, op0=ALU.mult),
                    waits=[("vec", v10)], inc="vec")
                vX = pb.now("vec")
                for a_, b_, c_ in NS_COEFS:
                    tT = mm5(pbig[:, 3584:3712].bitcast(F32), X, ident_f[:],
                             [("vec", vX), ("scl", pb.now("scl"))])
                    eT = evac(nsT[:], pbig[:, 3584:3712], [("tns", tT)])
                    mA = mm5(pbig[:, 3712:3840], nsT[:], nsT[:], [("scl", eT)])
                    eA = evac(nsA[:], pbig[:, 3712:3840], [("tns", mA)])
                    mA2 = mm5(pbig[:, 3840:3968], nsA[:], nsA[:], [("scl", eA)])
                    v11 = pb.now("vec")
                    pb.emit("vector", lambda e, c_=c_: e.tensor_scalar(
                        nsB[:], pbig[:, 3840:3968], c_, None, op0=ALU.mult),
                        waits=[("tns", mA2), ("vec", v11)], inc="vec")
                    v11 = pb.now("vec")
                    pb.emit("vector", lambda e, b_=b_: e.scalar_tensor_tensor(
                        nsB[:], nsA[:], b_, nsB[:], op0=ALU.mult, op1=ALU.add),
                        waits=[("vec", v11)], inc="vec")
                    v11 = pb.now("vec")
                    mM = mm5(pbig[:, 3968:4096], nsB[:], X, [("vec", v11)])
                    pb.emit("vector", lambda e, X=X, a_=a_: e.scalar_tensor_tensor(
                        X, X, a_, pbig[:, 3968:4096], op0=ALU.mult, op1=ALU.add),
                        waits=[("tns", mM)], inc="vec")
                    vX = pb.now("vec")
                ns_done.append(vX)
            vW = pb.now("vec")
            for wmi, dwi in ((0, 1), (1, 0), (2, 2)):
                pb.emit("vector", lambda e, wmi=wmi, dwi=dwi: e.tensor_tensor(
                    wma[:, 128 * wmi:128 * wmi + 128], wma[:, 128 * wmi:128 * wmi + 128],
                    dwf[:, 128 * dwi:128 * dwi + 128], op=ALU.add),
                    waits=[("vec", ns_done[dwi]), ("vec", vW)], inc="vec")
            vW = pb.now("vec")
            for wmi in range(3):
                pb.emit("vector", lambda e, wmi=wmi: e.tensor_tensor(
                    scr5[:, 0:128], wma[:, 128 * wmi:128 * wmi + 128],
                    wma[:, 128 * wmi:128 * wmi + 128], op=ALU.mult),
                    waits=[("vec", vW), ("scl", pb.now("scl"))], inc="vec")
                _vsq = pb.now("vec")
                pb.emit("vector", lambda e: e.tensor_reduce(
                    wn0[:, 5:6], scr5[:, 0:128], axis=AX.X, op=ALU.add),
                    waits=[("vec", _vsq)], inc="vec")
                sEv2 = pb.now("vec")
                sE = evac(wn0[:, 6:7], wn0[:, 5:6], [("vec", sEv2)], act=AF.Ln)
                pb.emit("scalar", lambda e: e.activation(
                    wn0[:, 6:7], wn0[:, 6:7], AF.Exp, scale=0.5),
                    waits=[("scl", sE)], inc="scl")
                sE = pb.now("scl")
                v12 = pb.now("vec")
                pb.emit("vector", lambda e: e.tensor_scalar(
                    wn0[:, 6:7], wn0[:, 6:7], 1e-5, None, op0=ALU.add),
                    waits=[("scl", sE), ("vec", v12)], inc="vec")
                v12 = pb.now("vec")
                pb.emit("vector", lambda e: e.reciprocal(wn0[:, 6:7], wn0[:, 6:7]),
                        waits=[("vec", v12)], inc="vec")
                v12 = pb.now("vec")
                pb.emit("vector", lambda e, wmi=wmi: e.tensor_scalar(
                    wn0[:, 6:7], wn0[:, 6:7], wn0[:, wmi:wmi + 1], None, op0=ALU.mult),
                    waits=[("vec", v12)], inc="vec")
                v12 = pb.now("vec")
                pb.emit("vector", lambda e, wmi=wmi: e.tensor_scalar(
                    wma[:, 128 * wmi:128 * wmi + 128], wma[:, 128 * wmi:128 * wmi + 128],
                    wn0[:, 6:7], None, op0=ALU.mult),
                    waits=[("vec", v12)], inc="vec")
                vW = pb.now("vec")
            pb.emit("vector", lambda e: e.tensor_copy(wmb[:], wma[:]),
                    waits=[("vec", vW)], inc="vec")
            vwb2 = pb.now("vec")
            for wi in range(3):
                po = pp[5][:, 0:64].bitcast(BF16)
                pb.emit("tensor", lambda e, wi=wi, po=po: e.transpose(
                    po, wmb[:, 128 * wi:128 * wi + 128], ident_b[:]),
                    waits=[("vec", vwb2), ("scl", pb.now("scl"))], inc="tns")
                t5 = pb.now("tns")
                pb.emit("scalar", lambda e, wi=wi, po=po: e.copy(
                    wmbT[:, 128 * wi:128 * wi + 128], po),
                    waits=[("tns", t5)], inc="scl")
            wn_done = pb.now("scl")
            f1 = forward(1, wn_done)
            if STAGE == "lct0" and j == 0:
                dbf = sbl(c5b, "dbf", 2048, F32)
                for half in range(2):
                    vv9 = pb.now("vec")
                    pb.emit("vector", lambda e, half=half: e.tensor_copy(
                        dbf[:], ofm[:, 2048 * half:2048 * half + 2048]),
                        waits=[("scl", f1), ("vec", vv9), ("dsc", pb.now("dsc"))], inc="vec")
                    vv9 = pb.now("vec")
                    pb.dma("scalar", g["out"][128 * half:128 * half + 128, :], dbf[:],
                           waits=[("vec", vv9)], inc="dsc")
                vv9 = pb.now("vec")
                pb.emit("vector", lambda e: e.tensor_copy(dbf[:, 0:640], smalls[:]),
                        waits=[("vec", vv9), ("dsc", pb.now("dsc"))], inc="vec")
                vv9 = pb.now("vec")
                pb.dma("scalar", g["out"][256:384, 0:640], dbf[:, 0:640],
                       waits=[("vec", vv9)], inc="dsc")
                pb.dma("scalar", g["out"][384:512, 0:384], wma[:],
                       waits=[("scl", f1), ("dsc", pb.now("dsc"))], inc="dsc")
                pb.dma("scalar", g["out"][512:640, 0:384], dwf[:],
                       waits=[("scl", f1), ("dsc", pb.now("dsc"))], inc="dsc")
                pb.dma("scalar", g["out"][512:640, 384:768], dbg_pre[:],
                       waits=[("scl", f1), ("dsc", pb.now("dsc"))], inc="dsc")
                for nm_, buf_, row_ in (("t1", t1q, 640), ("hb", hbq, 768), ("kp", kpq, 896)):
                    vv9 = pb.now("vec")
                    pb.emit("vector", lambda e, buf_=buf_: e.tensor_copy(
                        dbf[:], buf_[:, 0:2048]),
                        waits=[("scl", f1), ("vec", vv9), ("dsc", pb.now("dsc"))], inc="vec")
                    vv9 = pb.now("vec")
                    pb.dma("scalar", g["out"][row_:row_ + 128, :], dbf[:],
                           waits=[("vec", vv9)], inc="dsc")
                pb.flush("dbg")
                finish()
                return
            oev = {0: f1, 1: f1}
            oprev = f1
            otm_guard = pb.now("vec")
            for tt in range(32):
                po = pp[6 + (tt % 2)][:, 0:64].bitcast(BF16)
                pb.emit("tensor", lambda e, tt=tt, po=po: e.transpose(
                    po, ofm[:, 128 * tt:128 * tt + 128], ident_b[:]),
                    waits=[("scl", max(f1, oev[tt % 2]))], inc="tns")
                tA = pb.now("tns")
                eA = evac(otm[:], po, [("tns", tA), ("vec", otm_guard)])
                if STAGE == "lct1" and j == 0 and tt == 0:
                    pb.emit("vector", lambda e: e.tensor_copy(dbg_otm[:, 0:128], otm[:]),
                            waits=[("scl", eA), ("vec", pb.now("vec"))], inc="vec")
                v13 = pb.now("vec")
                pb.emit("vector", lambda e: e.tensor_reduce(
                    col_tmp[:, 24:25], otm[:], axis=AX.X, op=ALU.add),
                    waits=[("scl", eA), ("vec", v13)], inc="vec")
                v13 = pb.now("vec")
                pb.emit("vector", lambda e: e.tensor_tensor(
                    scr5[:, 0:128], otm[:], otm[:], op=ALU.mult),
                    waits=[("scl", eA), ("vec", v13)], inc="vec")
                _vsq = pb.now("vec")
                pb.emit("vector", lambda e: e.tensor_reduce(
                    col_tmp[:, 25:26], scr5[:, 0:128], axis=AX.X, op=ALU.add),
                    waits=[("vec", _vsq)], inc="vec")
                sB = pb.now("vec")
                pb.emit("vector", lambda e: e.tensor_scalar(
                    col_tmp[:, 24:25], col_tmp[:, 24:25], 1.0 / 128.0, None, op0=ALU.mult),
                    waits=[("vec", v13)], inc="vec")
                v13 = pb.now("vec")
                pb.emit("vector", lambda e: e.tensor_scalar(
                    col_tmp[:, 26:27], col_tmp[:, 24:25], col_tmp[:, 24:25], None,
                    op0=ALU.mult), waits=[("vec", v13)], inc="vec")
                v13 = pb.now("vec")
                pb.emit("vector", lambda e: e.scalar_tensor_tensor(
                    col_tmp[:, 25:26], col_tmp[:, 25:26], 1.0 / 128.0, col_tmp[:, 26:27],
                    op0=ALU.mult, op1=ALU.subtract),
                    waits=[("vec", max(v13, sB))], inc="vec")
                v13 = pb.now("vec")
                pb.emit("scalar", lambda e: e.activation(
                    col_tmp[:, 27:28], col_tmp[:, 25:26], AF.Ln, bias=ccol(1e-6)),
                    waits=[("vec", v13)], inc="scl")
                sB = pb.now("scl")
                pb.emit("scalar", lambda e: e.activation(
                    col_tmp[:, 27:28], col_tmp[:, 27:28], AF.Exp, scale=0.5),
                    waits=[("scl", sB)], inc="scl")
                sB = pb.now("scl")
                pb.emit("vector", lambda e: e.reciprocal(col_tmp[:, 27:28], col_tmp[:, 27:28]),
                        waits=[("scl", sB)], inc="vec")
                v13 = pb.now("vec")
                pb.emit("vector", lambda e: e.tensor_scalar(
                    otm[:], otm[:], col_tmp[:, 24:25], col_tmp[:, 27:28],
                    op0=ALU.subtract, op1=ALU.mult),
                    waits=[("vec", v13)], inc="vec")
                v13 = pb.now("vec")
                if STAGE == "lct1" and j == 0 and tt == 0:
                    pb.emit("vector", lambda e: e.tensor_copy(dbg_otm[:, 256:384], otm[:]),
                            waits=[("vec", v13)], inc="vec")
                    v13 = pb.now("vec")
                pb.emit("vector", lambda e: e.tensor_tensor(
                    otm[:], otm[:], grep_[:], op=ALU.mult),
                    waits=[("vec", v13), ("dls", pb.now("dls"))], inc="vec")
                v13 = pb.now("vec")
                if STAGE == "lct1" and j == 0 and tt == 0:
                    pb.emit("vector", lambda e: e.tensor_copy(dbg_otm[:, 384:512], otm[:]),
                            waits=[("vec", v13)], inc="vec")
                    pb.emit("vector", lambda e: e.tensor_copy(dbg_otm[:, 512:640], grep_[:]),
                            waits=[("vec", v13)], inc="vec")
                    v13 = pb.now("vec")
                pb.emit("vector", lambda e: e.tensor_tensor(
                    otm[:], otm[:], brep_[:], op=ALU.add),
                    waits=[("vec", v13)], inc="vec")
                v13 = pb.now("vec")
                if STAGE == "lct1" and j == 0 and tt == 0:
                    pb.emit("vector", lambda e: e.tensor_copy(dbg_otm[:, 640:768], otm[:]),
                            waits=[("vec", v13)], inc="vec")
                    v13 = pb.now("vec")
                pb.emit("vector", lambda e, tt=tt, j=j: e.tensor_scalar(
                    otm[:], otm[:], smalls[:, 20 * tt + 16 + j:20 * tt + 17 + j], None,
                    op0=ALU.mult), waits=[("vec", v13)], inc="vec")
                v13 = pb.now("vec")
                if STAGE == "lct1" and j == 0 and tt == 0:
                    pb.emit("vector", lambda e: e.tensor_copy(dbg_otm[:, 128:256], otm[:]),
                            waits=[("vec", v13)], inc="vec")
                    v13 = pb.now("vec")
                pB = pp[3][:, 0:128]
                pb.emit("tensor", lambda e, pB=pB: e.transpose(
                    pB.bitcast(F32R), otm[:], ident_r[:]),
                    waits=[("vec", v13), ("scl", oprev)], inc="tns")
                tB = pb.now("tns")
                otm_guard = v13
                oprev = evac(O1[j][:, 128 * tt:128 * tt + 128],
                             pB.bitcast(F32R), [("tns", tB)])
                oev[tt % 2] = oprev
        if STAGE == "lct1":
            dbf2 = sbl(c5b, "dbf2", 2048, F32)
            for half in range(2):
                vv9 = pb.now("vec")
                pb.emit("vector", lambda e, half=half: e.tensor_copy(
                    dbf2[:], O1[0][:, 2048 * half:2048 * half + 2048].bitcast(F32)),
                    waits=[("scl", pb.now("scl")), ("vec", vv9), ("dsc", pb.now("dsc"))], inc="vec")
                vv9 = pb.now("vec")
                pb.dma("scalar", g["out"][128 * half:128 * half + 128, :], dbf2[:],
                       waits=[("vec", vv9)], inc="dsc")
            pb.dma("scalar", g["out"][256:384, 0:768], dbg_otm[:],
                   waits=[("dsc", pb.now("dsc"))], inc="dsc")
            for half in range(2):
                vv9 = pb.now("vec")
                pb.emit("vector", lambda e, half=half: e.tensor_copy(
                    dbf2[:], O1[3][:, 2048 * half:2048 * half + 2048].bitcast(F32)),
                    waits=[("scl", pb.now("scl")), ("vec", vv9), ("dsc", pb.now("dsc"))], inc="vec")
                vv9 = pb.now("vec")
                pb.dma("scalar", g["out"][384 + 128 * half:384 + 128 * half + 128, :], dbf2[:],
                       waits=[("vec", vv9)], inc="dsc")
            pb.flush("dbg2")
            finish()
            return
        pb.flush("ph5b")

      with ExitStack() as c5c:
        wo1r = sbl(c5c, "wo1r", 4 * 2048, F32R)
        s1sb = sbl(c5c, "s1sb", 2 * 2048, BF16)
        wl9 = 0
        for hch in range(4):
            wl9 = pb.dma("sync", wo1r[:, 2048 * hch:2048 * hch + 2048],
                         g["wo1"][128 * hch:128 * hch + 128, :].bitcast(F32R), inc="dls")
        st9 = {0: 0, 1: 0}
        prev_ev9 = pb.now("scl")
        s1dst = g["s1_part"].rearrange("(c m h) k -> c h m k", m=128, h=16)
        for tt in range(32):
            for nb_ in range(4):
                for hch in range(4):
                    pb.emit("tensor", lambda e, tt=tt, nb_=nb_, hch=hch: e.matmul(
                        pp[nb_], O1[hch][:, 128 * tt:128 * tt + 128],
                        wo1r[:, 2048 * hch + 512 * nb_:2048 * hch + 512 * nb_ + 512],
                        start=(hch == 0), stop=(hch == 3)),
                        waits=[("dls", wl9), ("scl", prev_ev9)], inc="tns")
            mmd9 = pb.now("tns")
            dst9 = s1sb[:, (tt % 2) * 2048:(tt % 2) * 2048 + 2048]
            for nb_ in range(4):
                pb.emit("scalar", lambda e, nb_=nb_, dst9=dst9: e.copy(
                    dst9[:, 512 * nb_:512 * nb_ + 512], pp[nb_]),
                    waits=[("tns", mmd9), ("dt%d" % (tt % 2), st9[tt % 2])], inc="scl")
            prev_ev9 = pb.now("scl")
            st9[tt % 2] = pb.dma("scalar", s1dst[tt // 16, tt % 16],
                                 dst9, waits=[("scl", prev_ev9)], inc="dt%d" % (tt % 2))
        pb.emit("gpsimd", lambda e: e.collective_compute(
            "ReduceScatter", ALU.add, replica_groups=[[0, 1, 2, 3], [4, 5, 6, 7]],
            ins=[g["s1_part"].opt()], outs=[g["s1_out"].opt()]),
            waits=[("dt0", pb.now("dt0")), ("dt1", pb.now("dt1"))], inc="cc")
        cc_rs1 = pb.now("cc")
        pb.flush("ph5c")

    if STAGE == "lact":
        s1bh = sb("s1bh", 2048, BF16)
        s1bf = sb("s1bf", 2048, F32)
        pb.wait_only("sync", [("cc", cc_rs1)])
        SRC = g["s1_out"]
        for tt in range(8):
            vv9 = pb.now("vec")
            ld9 = pb.dma("sync", s1bh[:], SRC[128 * tt:128 * tt + 128, :],
                         waits=[("vec", vv9)], inc="dls")
            pb.emit("vector", lambda e: e.tensor_copy(s1bf[:], s1bh[:]),
                    waits=[("dls", ld9), ("dsc", pb.now("dsc"))], inc="vec")
            vv9 = pb.now("vec")
            pb.dma("scalar", g["out"][128 * tt:128 * tt + 128, :], s1bf[:],
                   waits=[("vec", vv9)], inc="dsc")
        finish()
        return

    with ExitStack() as c6:
        x1f = sbl(c6, "x1f", 2048, F32)
        y1h = sbl(c6, "y1h", 2048, BF16)
        s1h = sbl(c6, "s1h", 2048, BF16)
        g1f = sbl(c6, "g1f", 2048, F32)
        scr6 = sbl(c6, "scr6", 2048, F32)
        pb.wait_only("sync", [("cc", cc_rs1), ("cc", cc_rs_y)])
        prev6 = 0
        for tt in range(8):
            w6 = [("dsc", prev6), ("vec", pb.now("vec")), ("scl", pb.now("scl"))]
            l1_ = pb.dma("sync", x1f[:], g["x1_d"][128 * tt:128 * tt + 128, :],
                         waits=w6, inc="dls")
            l2_ = pb.dma("sync", y1h[:], g["y1_out"][128 * tt:128 * tt + 128, :],
                         waits=w6, inc="dls")
            l3_ = pb.dma("sync", s1h[:], g["s1_out"][128 * tt:128 * tt + 128, :],
                         waits=w6, inc="dls")
            ld6 = pb.now("dls")
            s7 = pb.now("scl")
            pb.emit("vector", lambda e: e.tensor_tensor(
                scr6[:], y1h[:], y1h[:], op=ALU.mult),
                waits=[("dls", ld6), ("scl", s7), ("vec", pb.now("vec"))], inc="vec")
            _vsq = pb.now("vec")
            pb.emit("vector", lambda e: e.tensor_reduce(
                col_tmp[:, 28:29], scr6[:], axis=AX.X, op=ALU.add),
                waits=[("vec", _vsq)], inc="vec")
            s7v = pb.now("vec")
            pb.emit("scalar", lambda e: e.activation(
                col_tmp[:, 29:30], col_tmp[:, 28:29], AF.Ln, scale=1.0 / E,
                bias=ccol(1e-6)), waits=[("vec", s7v)], inc="scl")
            s7 = pb.now("scl")
            pb.emit("scalar", lambda e: e.activation(
                col_tmp[:, 29:30], col_tmp[:, 29:30], AF.Exp, scale=0.5),
                waits=[("scl", s7)], inc="scl")
            s7 = pb.now("scl")
            v14 = pb.now("vec")
            pb.emit("vector", lambda e: e.reciprocal(col_tmp[:, 29:30], col_tmp[:, 29:30]),
                    waits=[("scl", s7), ("vec", v14)], inc="vec")
            v14 = pb.now("vec")
            pb.emit("vector", lambda e: e.tensor_scalar(
                g1f[:], y1h[:], col_tmp[:, 29:30], None, op0=ALU.mult),
                waits=[("vec", v14)], inc="vec")
            v14 = pb.now("vec")
            pb.emit("scalar", lambda e: e.activation(g1f[:], g1f[:], AF.Silu),
                    waits=[("vec", v14)], inc="scl")
            s7 = pb.now("scl")
            pb.emit("vector", lambda e: e.tensor_tensor(
                g1f[:], g1f[:], s1h[:], op=ALU.mult),
                waits=[("scl", s7)], inc="vec")
            v14 = pb.now("vec")
            pb.emit("vector", lambda e: e.tensor_tensor(
                x1f[:], x1f[:], g1f[:], op=ALU.add),
                waits=[("vec", v14)], inc="vec")
            v14 = pb.now("vec")
            prev6 = pb.dma("scalar", g["out"][128 * tt:128 * tt + 128, :], x1f[:],
                           waits=[("vec", v14)], inc="dsc")
    finish()
    return



def kernel(**inputs):
    ni = 6 * 128
    per_core = []
    for c in range(8):
        d = _prep_core_inputs(c, inputs)
        for k, axis in [("gpk", 1), ("upk", 1), ("gsc", 1), ("usc", 1)]:
            a = d[k]
            if a.shape[axis] < ni:
                pad = [(0, 0), (0, 0)]
                pad[axis] = (0, ni - a.shape[axis])
                d[k] = np.pad(a, pad)
        for k in ["dpk", "dsc"]:
            a = d[k]
            want = ni if k == "dpk" else ni // 128
            if a.shape[0] < want:
                d[k] = np.pad(a, [(0, want - a.shape[0]), (0, 0)])
        per_core.append(d)
    nc = build(ni)
    res = run_bass_kernel_spmd(nc, per_core, core_ids=list(range(8)))
    shards = [res.results[c]["out"] for c in range(8)]
    return np.concatenate(shards, axis=0).reshape(B, T, E)
